# version 14 (final)
# speedup vs baseline: 1.3598x; 1.0128x over previous
import numpy as np
import ml_dtypes
import concourse.bass as bass
import concourse.mybir as mybir
from concourse.bass_utils import run_bass_kernel_spmd

F32 = mybir.dt.float32
BF16 = mybir.dt.bfloat16
AF = mybir.ActivationFunctionType
ALU = mybir.AluOpType
AX = mybir.AxisListType

T = 4096
D = 1024
E = 2048
NST = 32
RNK = 64
NJ = 16
TC = 512
NCH = 8
HF = 2048
NH = 16
ED = 4096
NEX = 8
BF = ml_dtypes.bfloat16
EPS_LN = 1e-5
EPS_RMS = 1e-6

AB_N = 71728
AF_N = 8868
AB_TAIL = 67632

_FVALS = [1e-5, 1e-6, 1.0, -0.5, 1.0 / 1024.0, 0.3535533905932738]


class Plan:
    ENGS = ("sync", "vector", "scalar", "tensor", "gpsimd")

    def __init__(self):
        self.ev = {e: [] for e in self.ENGS}
        self.cnt = {}
        self.owner = {}
        self._pend = {e: [] for e in self.ENGS}

    def sem_def(self, name, owner):
        self.cnt[name] = 0
        self.owner[name] = owner

    def after(self, *names):
        if not names:
            names = tuple(self.cnt.keys())
        return [(n, self.cnt[n]) for n in names if self.cnt[n] > 0]

    def emit(self, eng, fn, waits=(), inc=None, inc_by=1):
        w = self._pend[eng] + list(waits)
        self._pend[eng] = []
        incs = []
        if inc is not None:
            self.cnt[inc] += inc_by
            incs = [(inc, inc_by)]
        self.ev[eng].append((fn, w, incs))

    def phase(self):
        m = self.after()
        for e in self.ENGS:
            self._pend[e] = list(m)


def _replay(plan, sems, engname):
    done = {}

    def _(eng):
        for fn, waits, incs in plan.ev[engname]:
            best = {}
            for n, v in waits:
                if plan.owner[n] == engname:
                    continue
                if done.get(n, -1) >= v:
                    continue
                if best.get(n, -1) < v:
                    best[n] = v
            for n, v in best.items():
                eng.wait_ge(sems[n], v)
                done[n] = v
            inst = fn(eng)
            for n, by in incs:
                assert inst is not None, "inc on op returning None"
                inst.then_inc(sems[n], by)
    return _


def _pp(a):
    return np.ascontiguousarray(a, dtype=np.float32)


def _dvec(a):
    a = np.asarray(a)
    n = a.shape[0] // 128
    return np.ascontiguousarray(a.reshape(n, 128).T, dtype=np.float32)


def _prep_consts():
    c = {}
    ident = np.eye(128, dtype=np.float32)
    c["IDF32"] = ident
    selq = np.zeros((128, 512), np.float32)
    for dq in range(4):
        for m in range(128):
            selq[dq * 32 + m % 32, dq * 128 + m] = 1.0
    c["SELQ32"] = selq
    selbc = np.zeros((64, 2048), np.float32)
    for ng in range(8):
        for m in range(128):
            selbc[ng * 4 + m // 32, (ng * 2 + 0) * 128 + m] = 1.0
            selbc[32 + ng * 4 + m // 32, (ng * 2 + 1) * 128 + m] = 1.0
    nsum = np.zeros((128, 32), np.float32)
    for k in range(128):
        nsum[k, k % 32] = 1.0
    cm = np.zeros((128, 896), np.float32)
    for p in range(128):
        cm[p, :] = (np.arange(896) >= p + 384)
    c["ONESR"] = np.ones((1, 128), np.float32)
    fv = np.zeros(8, np.float32)
    fv[:len(_FVALS)] = _FVALS
    c["FCONST"] = np.broadcast_to(fv, (128, 8)).copy()
    c["IDBF_f32"] = ident.copy()
    c["SELQBF_f32"] = selq.copy()
    c["SELBC_f32"] = selbc
    c["NSUM_f32"] = nsum
    c["CMASK_f32"] = cm
    c["ONESCB_f32"] = np.ones((128, 1), np.float32)
    return c


def _prep_core(inputs, b):
    g = inputs
    d = {"x": np.ascontiguousarray(np.asarray(g["L0_attn_x"])[b],
                                   dtype=np.float32)}
    for dirn, sfx in (("f", "_f"), ("b", "_b")):
        d[f"WinT_{dirn}"] = _pp(np.asarray(g["L0_attn_Win" + sfx]).T)
        d[f"WxT_{dirn}"] = _pp(np.asarray(g["L0_attn_Wx" + sfx]).T)
        d[f"WdtT_{dirn}"] = _pp(np.asarray(g["L0_attn_Wdt" + sfx]).T)
        d[f"WoutT_{dirn}"] = _pp(np.asarray(g["L0_attn_Wout" + sfx]).T)
        d[f"convw_{dirn}"] = np.ascontiguousarray(
            np.asarray(g["L0_attn_convw" + sfx]).reshape(16, 128, 4)
            .transpose(1, 0, 2), dtype=np.float32)
        d[f"convb_{dirn}"] = _dvec(g["L0_attn_convb" + sfx])
        d[f"bdt_{dirn}"] = _dvec(g["L0_attn_bdt" + sfx])
        d[f"Dp_{dirn}"] = _dvec(g["L0_attn_Dp" + sfx])
        d[f"lng_{dirn}"] = _dvec(g["L0_attn_ln_g" + sfx])
        d[f"lnb_{dirn}"] = _dvec(g["L0_attn_ln_b" + sfx])
        al = np.asarray(g["L0_attn_Alog" + sfx]).reshape(16, 4, 32, 8, 4)
        d[f"AlogL_{dirn}"] = np.ascontiguousarray(
            al.transpose(4, 2, 0, 1, 3).reshape(128, 512), dtype=np.float32)
    d["WfuseT"] = _pp(np.asarray(g["L0_attn_Wfuse"]).T)
    d["bfuse"] = _dvec(g["L0_attn_bfuse"])
    d["qkvT"] = _pp(np.asarray(g["L0_ffn_qkv_w"]).T)
    d["owT"] = _pp(np.asarray(g["L0_ffn_o_w"]).T)
    d["n1"] = _dvec(g["L0_ffn_n1_w"])
    d["n2"] = _dvec(g["L0_ffn_n2_w"])
    d["gateT"] = _pp(np.asarray(g["L0_ffn_gate_w"]).T)
    d["upT"] = _pp(np.asarray(g["L0_ffn_up_w"]).T)
    d["downT"] = _pp(np.asarray(g["L0_ffn_down_w"]).T)
    return d


def build(dbg=(), dirs=("f", "b"), phases=("mamba", "attn", "moe", "exit"), halves=(0, 1), p1c=True, lite1=False, kadump=False):
    nc = bass.Bass("TRN2", target_bir_lowering=False)
    IN = {}

    def di(name, shape):
        IN[name] = nc.dram_tensor(name, list(shape), F32,
                                  kind="ExternalInput").ap()

    di("x", (T, D))
    for dirn in ("f", "b"):
        di(f"WinT_{dirn}", (D, 2 * E))
        di(f"WxT_{dirn}", (E, 128))
        di(f"WdtT_{dirn}", (RNK, E))
        di(f"WoutT_{dirn}", (E, D))
        di(f"convw_{dirn}", (128, 16, 4))
        di(f"convb_{dirn}", (128, 16))
        di(f"bdt_{dirn}", (128, 16))
        di(f"Dp_{dirn}", (128, 16))
        di(f"lng_{dirn}", (128, 8))
        di(f"lnb_{dirn}", (128, 8))
        di(f"AlogL_{dirn}", (128, 512))
    di("WfuseT", (2 * D, D))
    di("bfuse", (128, 8))
    di("qkvT", (D, 3 * D))
    di("owT", (D, D))
    di("n1", (128, 8))
    di("n2", (128, 8))
    di("gateT", (D, NEX))
    di("upT", (D, ED))
    di("downT", (ED, D))
    di("IDF32", (128, 128))
    di("SELQ32", (128, 512))
    di("ONESR", (1, 128))
    di("FCONST", (128, 8))
    di("IDBF_f32", (128, 128))
    di("SELQBF_f32", (128, 512))
    di("SELBC_f32", (64, 2048))
    di("NSUM_f32", (128, 32))
    di("CMASK_f32", (128, 896))
    di("ONESCB_f32", (128, 1))

    OUT = nc.dram_tensor("out", [T, D], F32, kind="ExternalOutput").ap()

    SC = {
        "xF": nc.dram_tensor("xF", [D, T], F32).ap(),
        "x2F": nc.dram_tensor("x2F", [D, T], F32).ap(),
        "ln0F": nc.dram_tensor("ln0F", [D, T], BF16).ap(),
        "ucF": nc.dram_tensor("ucF", [E, T], BF16).ap(),
        "zsF": nc.dram_tensor("zsF", [E, T], BF16).ap(),
        "ygF": nc.dram_tensor("ygF", [E, T], BF16).ap(),
        "foF": nc.dram_tensor("foF", [D, T], BF16).ap(),
        "VTd": nc.dram_tensor("VTd", [32 * NH * 128, 66], BF16).ap(),
    }
    WBF = {}
    for nm, shape in [("WinT_f", (D, 2 * E)), ("WinT_b", (D, 2 * E)),
                      ("WoutT_f", (E, D)), ("WoutT_b", (E, D)),
                      ("WfuseT", (2 * D, D)), ("qkvT", (D, 3 * D)),
                      ("owT", (D, D)), ("upT", (D, ED)),
                      ("downT", (ED, D))]:
        WBF[nm] = nc.dram_tensor("bf_" + nm, list(shape), BF16).ap()

    DBG = {}
    for name, key in dbg:
        src = SC[key] if key in SC else WBF[key]
        DBG[name] = (nc.dram_tensor("dbg_" + name, list(src.shape),
                                    src.dtype, kind="ExternalOutput").ap(),
                     src)

    p = Plan()
    for s, o in [("d", "sync"), ("o", "gpsimd"), ("t", "tensor"),
                 ("v", "vector"), ("s", "scalar"), ("g", "gpsimd")]:
        p.sem_def(s, o)

    with (
        nc.sbuf_tensor("ABm", [128, AB_N], BF16) as ABm,
        nc.sbuf_tensor("AFm", [128, AF_N], F32) as AFm,
        nc.sbuf_tensor("CONST", [128, 780], F32) as CONST,
        nc.sbuf_tensor("CONSTB", [128, 3620], BF16) as CONSTB,
        nc.sbuf_tensor("PRM", [128, 1344], F32) as PRM,
        nc.sbuf_tensor("ROWS", [128, 3600], F32) as ROWS,
        nc.sbuf_tensor("CAR", [128, 561], F32) as CAR,
        nc.psum_tensor("PSA", [128, 2048], F32) as PSA,
        nc.psum_tensor("PSB", [128, 2048], F32) as PSB,
        nc.semaphore("sem_d") as sd,
        nc.semaphore("sem_o") as so,
        nc.semaphore("sem_t") as st_,
        nc.semaphore("sem_v") as sv,
        nc.semaphore("sem_s") as ss,
        nc.semaphore("sem_g") as sg,
        nc.Block() as block,
    ):
        sems = {"d": sd, "o": so, "t": st_, "v": sv, "s": ss, "g": sg}
        for k, val in enumerate(_FVALS):
            nc.const_aps.register(F32, val, CONST[:, 769 + k: 770 + k])
        _emit_all(nc, p, dict(AB=ABm, AFa=AFm, CONST=CONST, CONSTB=CONSTB,
                              PRM=PRM, ROWS=ROWS, CAR=CAR, PSA=PSA,
                              PSB=PSB),
                  IN, OUT, DBG, SC, WBF, dirs, phases, halves, p1c, lite1, kadump)
        for e in Plan.ENGS:
            getattr(block, e)(_replay(p, sems, e))
    return nc


def _dap(src_ap, offset, dims):
    return bass.AP(src_ap.tensor, src_ap.offset + offset, dims)


def _emit_all(nc, p, A, IN, OUT, DBG, SC, WBF, dirs, phases, halves, p1c, lite1, kadump):
    AB, AFa = A["AB"], A["AFa"]
    CONST, CONSTB, PRM = A["CONST"], A["CONSTB"], A["PRM"]
    ROWS, CAR, PSA, PSB = A["ROWS"], A["CAR"], A["PSA"], A["PSB"]

    IDF = CONST[:, 0:128]
    SELQ = CONST[:, 128:640]
    ONESR = CONST[0:1, 640:768]
    IDB = CONSTB[:, 0:128]
    SELQB = CONSTB[:, 128:640]
    SELBC = CONSTB[64:128, 640:2688]
    NSUM = CONSTB[:, 2688:2720]
    CMASK = CONSTB[:, 2720:3616]
    ONESCB = CONSTB[:, 3616:3617]

    PL = {}
    off = 0
    for dirn in ("f", "b"):
        for nm, w in [("convw", 64), ("convb", 16), ("bdt", 16),
                      ("Dp", 16), ("lng", 8), ("lnb", 8), ("Alane", 512)]:
            PL[f"{nm}_{dirn}"] = (off, w)
            off += w
    for nm, w in [("bfuse", 8), ("n1", 8), ("n2", 8), ("MK2", 16),
                  ("NMROW", 16)]:
        PL[nm] = (off, w)
        off += w
    assert off <= 1344, off

    def prm(nm):
        o, w = PL[nm]
        return PRM[:, o:o + w]

    def ld(dst, src, waits=(), slow=False):
        def f(e, a=dst, b=src, s=slow):
            if s:
                return e.dma_start(a, b, allow_slow_non_contiguous=True)
            return e.dma_start(a, b)
        p.emit("sync", f, waits=waits, inc="d", inc_by=16)

    def st(dst, src, waits=()):
        p.emit("gpsimd", lambda e, a=dst, b=src: e.dma_start(a, b),
               waits=waits, inc="o", inc_by=16)

    H = dict(nc=nc, p=p, AB=AB, AFa=AFa, CONST=CONST, CONSTB=CONSTB,
             PRM=PRM, ROWS=ROWS, CAR=CAR, PSA=PSA, PSB=PSB, IDF=IDF,
             SELQ=SELQ, ONESR=ONESR, IDB=IDB, SELQB=SELQB, SELBC=SELBC,
             NSUM=NSUM, CMASK=CMASK, ONESCB=ONESCB, prm=prm, ld=ld, st=st,
             IN=IN, SC=SC, WBF=WBF, dirs=dirs, phases=phases,
             halves=halves, p1c=p1c, lite1=lite1, kadump=kadump)

    _p0(H)
    p.phase()
    if "mamba" in H["phases"]:
        for dirn in H["dirs"]:
            _mamba_dir(H, dirn)
    if "attn" in H["phases"]:
        _attn(H)
    if "moe" in H["phases"]:
        _moe(H)
    if "exit" in H["phases"]:
        _exit(H, OUT)
    for name, (dst, src) in DBG.items():
        r, c = src.shape
        st(_dap(dst, 0, [[c, r], [1, c]]), _dap(src, 0, [[c, r], [1, c]]),
           waits=p.after())
    p.emit("gpsimd", lambda e: e.nop(), waits=p.after("o", "d"))


def _ln_stats(H, xb_bf, sq_bf, rstd, mrs=None, eps=EPS_LN, rms=False):
    p, PSB, ROWS, ONESCB = H["p"], H["PSB"], H["ROWS"], H["ONESCB"]
    vc, sc = p.cnt["v"], p.cnt["s"]
    if not rms:
        for i in range(8):
            p.emit("tensor", lambda e, i=i: e.matmul(
                PSB[0:1, 0:512], ONESCB, xb_bf[:, i * 512:(i + 1) * 512],
                start=(i == 0), stop=(i == 7)),
                waits=[("v", vc), ("s", sc)] if i == 0 else (), inc="t")
    for i in range(8):
        p.emit("tensor", lambda e, i=i: e.matmul(
            PSB[0:1, 512:1024], ONESCB, sq_bf[:, i * 512:(i + 1) * 512],
            start=(i == 0), stop=(i == 7)),
            waits=[("v", vc), ("s", sc)] if i == 0 else (), inc="t")
    tc_ = p.cnt["t"]
    if rms:
        p.emit("scalar", lambda e: e.activation(
            ROWS[0:1, 3072:3584], PSB[0:1, 512:1024], AF.Ln, bias=eps,
            scale=1.0 / D), waits=[("t", tc_)], inc="s")
        p.emit("scalar", lambda e: e.activation(
            rstd, ROWS[0:1, 3072:3584], AF.Exp, scale=-0.5), inc="s")
        return
    mean = ROWS[0:1, 0:512]
    m2 = ROWS[0:1, 2048:2560]
    var = ROWS[0:1, 2560:3072]
    p.emit("vector", lambda e: e.tensor_scalar_mul(
        mean, PSB[0:1, 0:512], 1.0 / D), waits=[("t", tc_)], inc="v")
    p.emit("vector", lambda e: e.tensor_mul(m2, mean, mean), inc="v")
    p.emit("vector", lambda e: e.scalar_tensor_tensor(
        var, PSB[0:1, 512:1024], 1.0 / D, m2, ALU.mult, ALU.subtract),
        inc="v")
    vc = p.cnt["v"]
    p.emit("scalar", lambda e: e.activation(
        ROWS[0:1, 3072:3584], var, AF.Ln, bias=eps), waits=[("v", vc)], inc="s")
    p.emit("scalar", lambda e: e.activation(
        rstd, ROWS[0:1, 3072:3584], AF.Exp, scale=-0.5), inc="s")
    sc = p.cnt["s"]
    p.emit("vector", lambda e: e.tensor_mul(mrs, mean, rstd),
           waits=[("s", sc)], inc="v")


def _p0(H):
    p, AB, AFa = H["p"], H["AB"], H["AFa"]
    CONST, CONSTB, PRM, PSA = H["CONST"], H["CONSTB"], H["PRM"], H["PSA"]
    IN, SC, WBF = H["IN"], H["SC"], H["WBF"]
    ld, st, prm, IDF = H["ld"], H["st"], H["prm"], H["IDF"]

    STG = AFa[:, 0:4096]
    STG2 = AFa[:, 4096:8192]
    STGB = AB[:, 0:4096]

    ld(CONST[:, 0:128], IN["IDF32"])
    ld(CONST[:, 128:640], IN["SELQ32"])
    ld(CONST[0:1, 640:768], IN["ONESR"])
    ld(CONST[:, 769:777], IN["FCONST"])
    for dirn in ("f", "b"):
        for nm in ("convw", "convb", "bdt", "Dp", "lng", "lnb"):
            ld(prm(f"{nm}_{dirn}"), IN[f"{nm}_{dirn}"])
    for nm in ("bfuse", "n1", "n2"):
        ld(prm(nm), IN[nm])
    bfc = [("IDBF_f32", CONSTB[:, 0:128], 128, 128),
           ("SELQBF_f32", CONSTB[:, 128:640], 128, 512),
           ("SELBC_f32", CONSTB[64:128, 640:2688], 64, 2048),
           ("NSUM_f32", CONSTB[:, 2688:2720], 128, 32),
           ("CMASK_f32", CONSTB[:, 2720:3616], 128, 896),
           ("ONESCB_f32", CONSTB[:, 3616:3617], 128, 1)]
    for name, dst, pn, w in bfc:
        ld(STG2[0:pn, 0:w], IN[name], waits=p.after("v"))
        dc = p.cnt["d"]
        p.emit("vector",
               lambda e, d=dst, s=STG2[0:pn, 0:w]: e.tensor_copy(d, s),
               waits=[("d", dc)], inc="v")
    for dirn in ("f", "b"):
        ld(STG2[:, 0:512], IN[f"AlogL_{dirn}"], waits=p.after("v", "s"))
        dc = p.cnt["d"]
        p.emit("scalar",
               lambda e, s=STG2[:, 0:512], t=STG2[:, 512:1024]:
               e.activation(t, s, AF.Exp), waits=[("d", dc)], inc="s")
        sc = p.cnt["s"]
        p.emit("vector",
               lambda e, d=prm(f"Alane_{dirn}"), t=STG2[:, 512:1024]:
               e.tensor_scalar_mul(d, t, -1.0), waits=[("s", sc)], inc="v")
    p.emit("vector", lambda e: e.memset(prm("MK2"), 0.0), inc="v")

    wlist = [("WinT_f", D, 2 * E), ("WinT_b", D, 2 * E),
             ("WoutT_f", E, D), ("WoutT_b", E, D), ("WfuseT", 2 * D, D),
             ("qkvT", D, 3 * D), ("owT", D, D), ("upT", D, ED),
             ("downT", ED, D)]
    STGs = (STG, STG2)
    STGBs = (STGB, AB[:, 4096:8192])
    cp_mark = {}
    st_mark = {}
    idx = 0
    for nm, K, M in wlist:
        for kb in range(K // 128):
            for c0 in range(0, M, 4096):
                w = min(4096, M - c0)
                sl = idx % 2
                idx += 1
                src = _dap(IN[nm], kb * 128 * M + c0, [[M, 128], [1, w]])
                dstd = _dap(WBF[nm], kb * 128 * M + c0, [[M, 128], [1, w]])
                ld(STGs[sl][:, 0:w], src,
                   waits=cp_mark.get(sl, p.after("v")))
                dc = p.cnt["d"]
                p.emit("vector", lambda e, a=STGBs[sl][:, 0:w],
                       b=STGs[sl][:, 0:w]: e.tensor_copy(a, b),
                       waits=[("d", dc)]
                       + st_mark.get(sl, [("o", p.cnt["o"])]), inc="v")
                cp_mark[sl] = [("v", p.cnt["v"])]
                st(dstd, STGBs[sl][:, 0:w], waits=cp_mark[sl])
                st_mark[sl] = [("o", p.cnt["o"])]

    XT = STG2
    for c in range(NCH):
        src = _dap(IN["x"], c * TC * D, [[D, 128], [128 * D, 4], [1, D]])
        ld(STG[:, 0:4096], src, waits=p.after("t", "v", "o"))
        dc = p.cnt["d"]
        for tb in range(4):
            for db in range(8):
                k = (tb * 8 + db) % 4
                pt = PSA[:, k * 512: k * 512 + 128]
                p.emit("tensor",
                       lambda e, o=pt, tb=tb, db=db:
                       e.transpose(o, STG[:, tb * 1024 + db * 128:
                                          tb * 1024 + db * 128 + 128],
                                   IDF),
                       waits=[("d", dc), ("s", p.cnt["s"])], inc="t")
                tc_ = p.cnt["t"]
                p.emit("scalar",
                       lambda e, o=XT[:, db * 512 + tb * 128:
                                      db * 512 + tb * 128 + 128], i=pt:
                       e.copy(o, i), waits=[("t", tc_)], inc="s")
        sc = p.cnt["s"]
        st(_dap(SC["xF"], c * TC, [[T, 128], [128 * T, 8], [1, TC]]),
           XT[:, 0:4096], waits=[("s", sc)])


def _mamba_dir(H, dirn):
    p, AB, AFa = H["p"], H["AB"], H["AFa"]
    ROWS, CAR, PSA, PSB = H["ROWS"], H["CAR"], H["PSA"], H["PSB"]
    IN, SC, WBF = H["IN"], H["SC"], H["WBF"]
    ld, st, prm = H["ld"], H["st"], H["prm"]
    SELQ, SELQB, SELBC, NSUM = H["SELQ"], H["SELQB"], H["SELBC"], H["NSUM"]
    ONESR, ONESCB = H["ONESR"], H["ONESCB"]

    WIN = AB[:, 0:32768]
    LN0 = AB[:, 32768:36864]
    LR = AB[:, 36864:40960]
    UC = AB[:, 40960:49152]
    ZS = AB[:, 49152:57344]
    SQ = ZS[:, 0:4096]
    UB = AB[:, 57344:65584]
    WX = AB[:, 65584:67632]
    DTR = AB[0:64, AB_TAIL:AB_TAIL + 4096]
    BCR = AB[64:128, AB_TAIL:AB_TAIL + 4096]
    XS = AFa[:, 0:4096]
    CB = AFa[:, 4096:4608]

    win_nm = f"WinT_{dirn}"
    cw = prm(f"convw_{dirn}")
    cb_ = prm(f"convb_{dirn}")
    bdt = prm(f"bdt_{dirn}")
    Dp = prm(f"Dp_{dirn}")
    Alane = prm(f"Alane_{dirn}")

    ld(WIN, _dap(WBF[win_nm], 0, [[2 * E, 128], [128 * 2 * E, 8],
                                  [1, 4096]]), waits=p.after())
    ld(AFa[:, 4608:6656], _dap(IN[f"WxT_{dirn}"], 0,
                               [[128, 128], [128 * 128, 16], [1, 128]]))
    dc = p.cnt["d"]
    p.emit("vector", lambda e, a=WX, b=AFa[:, 4608:6656]:
           e.tensor_copy(a, b), waits=[("d", dc)], inc="v")
    p.emit("vector", lambda e: e.memset(CAR[:, 512:561], 0.0), inc="v")

    LNT = AFa[:, 4608:5120]
    xs_free = p.after("v", "s", "t", "o")
    ln0_free = []
    lr_free = []
    stats_t = []
    silu_prev = p.after("o", "s")
    bank_mark = {}
    uspill = []
    zspill = []
    xdbl15 = []
    silu_mark = {}
    for c in range(NCH):
        if dirn == "f":
            ld(XS, _dap(SC["xF"], c * TC, [[T, 128], [128 * T, 8],
                                           [1, TC]]), waits=xs_free)
            dc = p.cnt["d"]
            p.emit("vector", lambda e: e.tensor_copy(LR, XS),
                   waits=[("d", dc)] + stats_t, inc="v")
            p.emit("scalar", lambda e: e.activation(SQ, XS, AF.Square),
                   waits=[("d", dc)] + zspill, inc="s")
            rstd = ROWS[0:1, 2048:2560]
            mrs = ROWS[0:1, 2560:3072]
            _ln_stats(H, LR, SQ, rstd, mrs)
            stats_t = [("t", p.cnt["t"])]
            vc, sc = p.cnt["v"], p.cnt["s"]
            p.emit("tensor", lambda e: e.matmul(
                PSA[:, 0:512], ONESR, rstd, start=True, stop=True),
                waits=[("v", vc), ("s", sc)], inc="t")
            p.emit("tensor", lambda e: e.matmul(
                PSA[:, 512:1024], ONESR, mrs, start=True, stop=True),
                inc="t")
            tc_ = p.cnt["t"]
            for i in range(8):
                p.emit("vector", lambda e, i=i: e.tensor_mul(
                    LNT, XS[:, i * 512:(i + 1) * 512], PSA[:, 0:512]),
                    waits=([("t", tc_)] + ln0_free) if i == 0
                    else [("t", tc_)], inc="v")
                p.emit("vector", lambda e, i=i: e.tensor_sub(
                    LN0[:, i * 512:(i + 1) * 512], LNT,
                    PSA[:, 512:1024]), inc="v")
            ln0_ready = p.cnt["v"]
            xs_free = [("v", ln0_ready), ("s", p.cnt["s"])]
            st(_dap(SC["ln0F"], c * TC, [[T, 128], [128 * T, 8], [1, TC]]),
               LN0, waits=[("v", ln0_ready)])
            ln0_st = [("o", p.cnt["o"])]
        else:
            cg = NCH - 1 - c
            ld(LR, _dap(SC["ln0F"], cg * TC, [[T, 128], [128 * T, 8],
                                              [1, TC]]),
               waits=lr_free if lr_free else p.after("v", "s", "t", "o"))
            dc = p.cnt["d"]
            for i in range(8):
                rev = bass.AP(LR.tensor, LR.offset + i * 512 + 511,
                              [[AB_N, 128], [-1, 512]])
                p.emit("vector", lambda e, i=i, r=rev: e.tensor_copy(
                    LN0[:, i * 512:(i + 1) * 512], r),
                    waits=([("d", dc)] + ln0_free) if i == 0
                    else [("d", dc)], inc="v")
            ln0_ready = p.cnt["v"]
            lr_free = [("v", ln0_ready)]
            ln0_st = []

        def xz_group(eb, first_w):
            bank = PSA[:, (eb % 4) * 512:((eb % 4) + 1) * 512]
            for kb in range(8):
                p.emit("tensor", lambda e, eb=eb, kb=kb, bk=bank:
                       e.matmul(
                           bk, WIN[:, kb * 4096 + eb * 128:
                                   kb * 4096 + eb * 128 + 128],
                           LN0[:, kb * 512:(kb + 1) * 512],
                           start=(kb == 0), stop=(kb == 7)),
                       waits=first_w if kb == 0 else (),
                       inc="t" if kb == 7 else None)
            return bank

        def x_dbl(kb):
            p.emit("tensor", lambda e, kb=kb: e.matmul(
                PSB[:, 1024:1536], WX[:, kb * 128:(kb + 1) * 128],
                UC[:, kb * 512:(kb + 1) * 512],
                start=(kb == 0), stop=(kb == 15)),
                waits=silu_mark[kb], inc="t" if kb == 15 else None)

        for sl in range(16):
            fw = bank_mark.pop(sl % 4, None)
            if fw is None:
                fw = p.after("s", "v")
            if sl == 0:
                fw = fw + [("v", ln0_ready)]
            bank = xz_group(sl, fw)
            tg = p.cnt["t"]
            p.emit("vector", lambda e, sl=sl, bk=bank: e.tensor_copy(
                UB[:, sl * 515 + 3: sl * 515 + 515], bk),
                waits=[("t", tg)], inc="v")
            bank_mark[sl % 4] = [("v", p.cnt["v"])]
            p.emit("vector", lambda e, j=sl: e.tensor_copy(
                UB[:, j * 515: j * 515 + 3],
                CAR[:, 512 + j * 3: 512 + j * 3 + 3]), inc="v")
            p.emit("vector", lambda e, j=sl: e.tensor_scalar(
                CB, UB[:, j * 515: j * 515 + 512],
                cw[:, j * 4: j * 4 + 1], cb_[:, j: j + 1],
                ALU.mult, ALU.add),
                waits=silu_prev if sl == 0
                else silu_mark[sl - 1], inc="v")
            for k in range(1, 4):
                p.emit("vector", lambda e, j=sl, k=k:
                       e.scalar_tensor_tensor(
                           CB, UB[:, j * 515 + k: j * 515 + k + 512],
                           cw[:, j * 4 + k: j * 4 + k + 1], CB,
                           ALU.mult, ALU.add), inc="v")
            p.emit("vector", lambda e, j=sl: e.tensor_copy(
                CAR[:, 512 + j * 3: 512 + j * 3 + 3],
                UB[:, j * 515 + 512: j * 515 + 515]), inc="v")
            vc = p.cnt["v"]
            sw = [("v", vc)]
            if sl == 0:
                sw = sw + xdbl15 + uspill
            p.emit("scalar", lambda e, j=sl: e.activation(
                UC[:, j * 512:(j + 1) * 512], CB, AF.Silu),
                waits=sw, inc="s")
            silu_mark[sl] = [("s", p.cnt["s"])]
            if sl >= 2:
                x_dbl(sl - 2)
        x_dbl(14)
        x_dbl(15)
        tc_ = p.cnt["t"]
        xdbl15 = [("t", tc_)]
        p.emit("scalar", lambda e, c=c: e.copy(
            DTR[:, c * 512:(c + 1) * 512], PSB[0:64, 1024:1536]),
            waits=[("t", tc_)], inc="s")
        p.emit("scalar", lambda e, c=c: e.copy(
            BCR[:, c * 512:(c + 1) * 512], PSB[64:128, 1024:1536]),
            inc="s")
        sc = p.cnt["s"]
        st(_dap(SC["ucF"], c * TC, [[T, 128], [128 * T, 16], [1, TC]]),
           UC, waits=[("s", sc)])
        uspill = [("o", p.cnt["o"])]
        for eb in range(16, 32):
            fw = bank_mark.pop(eb % 4)
            bank = xz_group(eb, fw)
            tg = p.cnt["t"]
            zw = [("t", tg)]
            if eb == 16:
                zw = zw + zspill
            p.emit("scalar", lambda e, eb=eb, bk=bank: e.activation(
                ZS[:, (eb - 16) * 512:(eb - 15) * 512], bk, AF.Silu),
                waits=zw, inc="s")
            bank_mark[eb % 4] = [("s", p.cnt["s"])]
        ln0_free = [("t", p.cnt["t"])] + ln0_st
        sc = p.cnt["s"]
        st(_dap(SC["zsF"], c * TC, [[T, 128], [128 * T, 16], [1, TC]]),
           ZS[:, 0:8192], waits=[("s", sc)])
        zspill = [("o", p.cnt["o"])]
        silu_prev = silu_mark[15]
    p.phase()

    BREP = [AB[:, ng * 2048:(ng + 1) * 2048] for ng in range(8)]
    CREP = [AB[:, 16384 + ng * 2048: 16384 + (ng + 1) * 2048]
            for ng in range(8)]
    DTU4 = AB[:, 32768:40960]
    BBUF = [AB[:, 40960 + i * 2048: 43008 + i * 2048] for i in range(2)]
    HBUF = [AB[:, 45056 + i * 2048: 47104 + i * 2048] for i in range(2)]
    YBUF = [AB[:, 49152 + i * 2048: 51200 + i * 2048] for i in range(2)]
    UCJ2 = [AB[:, 53248:55296], AB[:, 63488:65536]]
    ZSJ2 = [AB[:, 55296:57344], AB[:, 65536:67584]]
    YG = AB[:, 57344:59392]
    DTUB = AB[:, 59392:61440]
    WDT = AB[0:64, 61440:63488]
    ZF = AFa[:, 0:2048]
    NZ = AFa[:, 2048:4096]
    DTJ = AFa[:, 4096:6144]
    ABUF = [AFa[:, 6144:8192], AFa[:, 2048:4096]]
    Y2 = ZF

    ld(AFa[0:64, 0:2048], _dap(IN[f"WdtT_{dirn}"], 0,
                                   [[E, 64], [1, 2048]]),
       waits=p.after())
    dc = p.cnt["d"]
    p.emit("vector", lambda e, a=WDT, b=AFa[0:64, 0:2048]:
           e.tensor_copy(a, b), waits=[("d", dc)], inc="v")
    p.phase()

    for hf in H["halves"]:
        for ng in range(8):
            for which, dst in ((0, BREP[ng]), (1, CREP[ng])):
                for s4 in range(4):
                    p.emit("tensor", lambda e, ng=ng, w=which, s4=s4,
                           hf=hf: e.matmul(
                               PSA[:, s4 * 512:(s4 + 1) * 512],
                               SELBC[:, (ng * 2 + w) * 128:
                                     (ng * 2 + w) * 128 + 128],
                               BCR[:, hf * 2048 + s4 * 512:
                                   hf * 2048 + s4 * 512 + 512],
                               start=True, stop=True),
                           waits=p.after("s", "v", "g") if s4 == 0
                           else (), inc="t")
                tc_ = p.cnt["t"]
                p.emit("scalar", lambda e, d=dst: e.copy(d, PSA[:, 0:2048]),
                       waits=[("t", tc_)], inc="s")
        p.phase()

        ucj_guard = {}
        for j in range(NJ):
            UCJ = UCJ2[j % 2]
            ZSJ = ZSJ2[j % 2]
            ld(UCJ, _dap(SC["ucF"], j * 128 * T + hf * HF,
                         [[T, 128], [1, HF]]),
               waits=ucj_guard.get(j % 2, []))
            ld(ZSJ, _dap(SC["zsF"], j * 128 * T + hf * HF,
                         [[T, 128], [1, HF]]),
               waits=ucj_guard.get(j % 2, []))
            dcj = p.cnt["d"]
            for s4 in range(4):
                p.emit("tensor", lambda e, s4=s4, j=j, hf=hf: e.matmul(
                    PSA[:, s4 * 512:(s4 + 1) * 512],
                    WDT[:, j * 128:(j + 1) * 128],
                    DTR[:, hf * 2048 + s4 * 512:
                        hf * 2048 + s4 * 512 + 512],
                    start=True, stop=True),
                    waits=p.after("s", "v") if s4 == 0 else (), inc="t")
                tc_ = p.cnt["t"]
                p.emit("scalar", lambda e, s4=s4, j=j: e.activation(
                    ZF[:, s4 * 512:(s4 + 1) * 512],
                    PSA[:, s4 * 512:(s4 + 1) * 512], AF.Identity,
                    bias=bdt[:, j: j + 1]), waits=[("t", tc_)], inc="s")
            sc = p.cnt["s"]
            p.emit("vector", lambda e: e.tensor_single_scalar(
                NZ, ZF, -1.0, ALU.mult),
                waits=[("s", sc)] + p.after("t"), inc="v")
            p.emit("vector", lambda e: e.tensor_tensor(
                NZ, ZF, NZ, ALU.min), inc="v")
            vc = p.cnt["v"]
            p.emit("scalar", lambda e: e.activation(NZ, NZ, AF.Exp),
                   waits=[("v", vc)], inc="s")
            p.emit("scalar", lambda e: e.activation(NZ, NZ, AF.Ln,
                                                    bias=1.0), inc="s")
            sc = p.cnt["s"]
            p.emit("vector", lambda e: e.scalar_tensor_tensor(
                DTJ, ZF, 0.0, NZ, ALU.max, ALU.add),
                waits=[("s", sc)], inc="v")
            p.emit("vector", lambda e, UCJ=UCJ: e.tensor_mul(
                DTUB, DTJ, UCJ), waits=[("d", dcj)], inc="v")
            vdt = p.cnt["v"]
            if hf == 1 and H.get("lite1"):
                p.phase()
                continue
            tdq_marks = {}
            aexp7_pred = {}

            def emit_dtrep(dq, w):
                for s4 in range(4):
                    p.emit("tensor", lambda e, dq=dq, s4=s4: e.matmul(
                        PSA[:, s4 * 512:(s4 + 1) * 512],
                        SELQ[:, dq * 128:(dq + 1) * 128],
                        DTJ[:, s4 * 512:(s4 + 1) * 512],
                        start=True, stop=True),
                        waits=w if s4 == 0 else (), inc="t")
                tdq_marks[dq] = p.cnt["t"]

            emit_dtrep(0, [("v", vdt)] + p.after("s"))
            prev_lo = None
            prev_hi = None
            for dq in range(4):
                for s4 in range(2):
                    p.emit("tensor", lambda e, dq=dq, s4=s4: e.matmul(
                        PSB[:, s4 * 512:(s4 + 1) * 512],
                        SELQB[:, dq * 128:(dq + 1) * 128],
                        DTUB[:, s4 * 512:(s4 + 1) * 512],
                        start=True, stop=True),
                        waits=(([("v", vdt)] + p.after("s", "g"))
                               if dq == 0 and s4 == 0 else
                               (prev_lo if s4 == 0 else ())), inc="t")
                tlo = p.cnt["t"]
                for s4 in range(2, 4):
                    p.emit("tensor", lambda e, dq=dq, s4=s4: e.matmul(
                        PSB[:, s4 * 512:(s4 + 1) * 512],
                        SELQB[:, dq * 128:(dq + 1) * 128],
                        DTUB[:, s4 * 512:(s4 + 1) * 512],
                        start=True, stop=True),
                        waits=(prev_hi if s4 == 2 and dq > 0 else ()),
                        inc="t")
                thi = p.cnt["t"]
                p.emit("scalar", lambda e, dq=dq: e.copy(
                    DTU4[:, dq * 2048: dq * 2048 + 1024],
                    PSB[:, 0:1024]), waits=[("t", tlo)], inc="s")
                prev_lo = [("s", p.cnt["s"])]
                p.emit("scalar", lambda e, dq=dq: e.copy(
                    DTU4[:, dq * 2048 + 1024:(dq + 1) * 2048],
                    PSB[:, 1024:2048]), waits=[("t", thi)], inc="s")
                prev_hi = [("s", p.cnt["s"])]
            it = 0
            scan_marks = {}
            yred_marks = {}
            for dq in range(4):
                tdq = tdq_marks[dq]
                for ng in range(8):
                    u = (j * 4 + dq) * 8 + ng
                    sl = it % 2
                    it += 1
                    p.emit("scalar", lambda e, u=u, sl=sl: e.activation(
                        ABUF[sl], PSA[:, 0:2048], AF.Exp,
                        scale=Alane[:, u: u + 1]),
                        waits=[("t", tdq)] + scan_marks.get(it - 2, []),
                        inc="s")
                    sa = p.cnt["s"]
                    if ng == 7:
                        assert aexp7_pred.pop(dq, sa) == sa
                    p.emit("vector", lambda e, dq=dq, ng=ng, sl=sl:
                           e.tensor_mul(BBUF[sl],
                                        DTU4[:, dq * 2048:
                                             (dq + 1) * 2048],
                                        BREP[ng]),
                           waits=[("s", sa)], inc="v")
                    ini = 0.0 if hf == 0 else CAR[:, u: u + 1]
                    p.emit("vector", lambda e, sl=sl, ini=ini:
                           e.tensor_tensor_scan(HBUF[sl], ABUF[sl],
                                                BBUF[sl], ini,
                                                ALU.mult, ALU.add),
                           waits=p.after("s"), inc="v")
                    scan_marks[it] = [("v", p.cnt["v"])]
                    vh = p.cnt["v"]
                    if hf == 0:
                        p.emit("scalar", lambda e, u=u, sl=sl: e.copy(
                            CAR[:, u: u + 1], HBUF[sl][:, HF - 1: HF]),
                            waits=[("v", vh)], inc="s")
                    p.emit("vector", lambda e, ng=ng, sl=sl: e.tensor_mul(
                        YBUF[sl], HBUF[sl], CREP[ng]),
                        waits=yred_marks.get(it - 2, []), inc="v")
                    vy = p.cnt["v"]
                    for s4 in range(4):
                        p.emit("tensor", lambda e, dq=dq, s4=s4, ng=ng,
                               sl=sl: e.matmul(
                                   PSB[32 * dq: 32 * dq + 32,
                                       s4 * 512:(s4 + 1) * 512], NSUM,
                                   YBUF[sl][:, s4 * 512:(s4 + 1) * 512],
                                   start=(ng == 0), stop=(ng == 7),
                                   tile_position=(0, 32 * dq)),
                               waits=[("v", vy)] if s4 == 0 else (),
                               inc="t" if s4 == 3 else None)
                    yred_marks[it] = [("t", p.cnt["t"])]
                    if ng == 4 and dq < 3:
                        pred = p.cnt["s"] + (5 if hf == 0 else 3)
                        aexp7_pred[dq] = pred
                        emit_dtrep(dq + 1, [("s", pred)])
            ty = p.cnt["t"]
            p.emit("vector", lambda e, j=j, UCJ=UCJ: e.scalar_tensor_tensor(
                Y2, UCJ, Dp[:, j: j + 1], PSB[:, 0:2048], ALU.mult,
                ALU.add), waits=[("t", ty)], inc="v")
            p.emit("vector", lambda e, ZSJ=ZSJ: e.tensor_mul(YG, Y2, ZSJ),
                   waits=p.after("o"), inc="v")
            vc = p.cnt["v"]
            ucj_guard[j % 2] = [("v", vc)]
            st(_dap(SC["ygF"], j * 128 * T + hf * HF, [[T, 128], [1, HF]]),
               YG, waits=[("v", vc)])

    WOUT = AB[:, 0:16384]
    YGC = AB[:, 16384:24576]
    FOLN = AB[:, 24576:28672]
    FOB = AB[:, 28672:32768]
    SQ2 = AB[:, 32768:36864]
    WFUSE = AB[:, 36864:53248]
    FOLD = AB[:, 53248:57344]
    BOREV = AB[:, 57344:61440]
    FO = AFa[:, 0:4096]
    XC = AFa[:, 4096:8192]
    TMP = AFa[:, 8192:8704]

    if not H["p1c"]:
        return
    ld(WOUT, _dap(WBF["WoutT_" + dirn], 0, [[D, 128], [128 * D, 16],
                                            [1, 1024]]), waits=p.after())
    if dirn == "b":
        ld(WFUSE, _dap(WBF["WfuseT"], 0, [[D, 128], [128 * D, 16],
                                          [1, 1024]]))
    gcol = prm(f"lng_{dirn}")
    bcol = prm(f"lnb_{dirn}")
    bfu = prm("bfuse")
    TMP2 = AB.bitcast(F32)[:, 30720:31232]
    p.phase()
    ygc_free = p.after("t")
    fo_free = []
    fob_free = []
    fost = []
    xcld = []
    bank_free = {}
    for c in range(NCH):
        ld(YGC, _dap(SC["ygF"], c * TC, [[T, 128], [128 * T, 16],
                                         [1, TC]]), waits=ygc_free)
        dc = p.cnt["d"]
        for db in range(8):
            bank = PSA[:, (db % 4) * 512:((db % 4) + 1) * 512]
            for kb in range(NJ):
                p.emit("tensor", lambda e, db=db, kb=kb, bk=bank: e.matmul(
                    bk, WOUT[:, kb * 1024 + db * 128: kb * 1024 + db * 128
                             + 128], YGC[:, kb * 512:(kb + 1) * 512],
                    start=(kb == 0), stop=(kb == 15)),
                    waits=([("d", dc)]
                           + bank_free.get(db % 4, p.after("s", "v")))
                    if kb == 0 else (), inc="t" if kb == 15 else None)
            tc_ = p.cnt["t"]
            p.emit("scalar", lambda e, db=db, bk=bank: e.copy(
                FO[:, db * 512:(db + 1) * 512], bk),
                waits=[("t", tc_)] + (fo_free if db == 0 else []),
                inc="s")
            bank_free[db % 4] = [("s", p.cnt["s"])]
        ygc_free = [("t", p.cnt["t"])]
        sc = p.cnt["s"]
        p.emit("vector", lambda e: e.tensor_copy(FOB, FO),
               waits=[("s", sc)] + fob_free, inc="v")
        p.emit("scalar", lambda e: e.activation(SQ2, FO, AF.Square),
               inc="s")
        rstd = ROWS[0:1, 2048:2560]
        mrs = ROWS[0:1, 2560:3072]
        _ln_stats(H, FOB, SQ2, rstd, mrs)
        fob_free = [("t", p.cnt["t"])]
        vc, sc = p.cnt["v"], p.cnt["s"]
        p.emit("tensor", lambda e: e.matmul(
            PSB[:, 1024:1536], ONESR, rstd, start=True, stop=True),
            waits=[("v", vc), ("s", sc)], inc="t")
        p.emit("tensor", lambda e: e.matmul(
            PSB[:, 1536:2048], ONESR, mrs, start=True, stop=True), inc="t")
        tc_ = p.cnt["t"]
        for db in range(8):
            p.emit("vector", lambda e, db=db: e.tensor_mul(
                TMP, FO[:, db * 512:(db + 1) * 512], PSB[:, 1024:1536]),
                waits=[("t", tc_)] + (fost if db == 0 else []), inc="v")
            p.emit("vector", lambda e: e.tensor_sub(
                TMP, TMP, PSB[:, 1536:2048]), inc="v")
            p.emit("vector", lambda e, db=db: e.tensor_scalar(
                FOLN[:, db * 512:(db + 1) * 512], TMP,
                gcol[:, db: db + 1], bcol[:, db: db + 1],
                ALU.mult, ALU.add), inc="v")
        vc = p.cnt["v"]
        fo_free = [("v", vc)]
        if dirn == "f":
            st(_dap(SC["foF"], c * TC, [[T, 128], [128 * T, 8], [1, TC]]),
               FOLN, waits=[("v", vc)])
            fost = [("o", p.cnt["o"])]
        else:
            cg = NCH - 1 - c
            for i in range(8):
                rev = bass.AP(FOLN.tensor, FOLN.offset + i * 512 + 511,
                              [[AB_N, 128], [-1, 512]])
                p.emit("vector", lambda e, i=i, r=rev: e.tensor_copy(
                    BOREV[:, i * 512:(i + 1) * 512], r),
                    waits=xcld if i == 0 else (), inc="v")
            ld(FOLD, _dap(SC["foF"], cg * TC, [[T, 128], [128 * T, 8],
                                               [1, TC]]), waits=xcld)
            ld(XC, _dap(SC["xF"], cg * TC, [[T, 128], [128 * T, 8],
                                            [1, TC]]))
            dc = p.cnt["d"]
            vrev = p.cnt["v"]
            tmps = (TMP, TMP2)
            add_mark = {}
            for db in range(8):
                bank = PSA[:, (db % 4) * 512:((db % 4) + 1) * 512]
                for kb in range(16):
                    rhs = (FOLD[:, kb * 512:(kb + 1) * 512] if kb < 8
                           else BOREV[:, (kb - 8) * 512:(kb - 7) * 512])
                    p.emit("tensor", lambda e, kb=kb, db=db, bk=bank,
                           rhs=rhs: e.matmul(
                               bk, WFUSE[:, kb * 1024 + db * 128:
                                         kb * 1024 + db * 128 + 128],
                               rhs, start=(kb == 0), stop=(kb == 15)),
                           waits=(([("d", dc), ("v", vrev)]
                                   + bank_free.get(db % 4, []))
                                  if kb == 0 else ()),
                           inc="t" if kb == 15 else None)
                tc_ = p.cnt["t"]
                tb = tmps[db % 2]
                p.emit("scalar", lambda e, db=db, bk=bank, tb=tb:
                       e.activation(tb, bk, AF.Identity,
                                    bias=bfu[:, db: db + 1]),
                       waits=[("t", tc_)] + add_mark.get(db - 2, []),
                       inc="s")
                bank_free[db % 4] = [("s", p.cnt["s"])]
                sc2 = p.cnt["s"]
                p.emit("vector", lambda e, db=db, tb=tb: e.tensor_add(
                    XC[:, db * 512:(db + 1) * 512],
                    XC[:, db * 512:(db + 1) * 512], tb),
                    waits=[("s", sc2)], inc="v")
                add_mark[db] = [("v", p.cnt["v"])]
            vc = p.cnt["v"]
            st(_dap(SC["xF"], cg * TC, [[T, 128], [128 * T, 8], [1, TC]]),
               XC, waits=[("v", vc)])
            xcld = [("v", vc), ("o", p.cnt["o"])]
    p.phase()


def _attn(H):
    p, AB, AFa = H["p"], H["AB"], H["AFa"]
    ROWS, PSA, PSB = H["ROWS"], H["PSA"], H["PSB"]
    IN, SC, WBF = H["IN"], H["SC"], H["WBF"]
    ld, st, prm = H["ld"], H["st"], H["prm"]
    IDF, IDB, ONESR, ONESCB, CMASK = (H["IDF"], H["IDB"], H["ONESR"],
                                      H["ONESCB"], H["CMASK"])

    KA = AB[:, 0:32768]
    QKVW = [AB[:, 32768:35840], AB[:, 35840:38912]]
    QS = AB[:, 38912:43008]
    LN1 = AB[:, 43008:47104]
    OWT = AB[:, 47104:55296]
    ATO = AB[:, 55296:59392]
    VSTG = AB[:, 59392:63488]
    VTS = AB[:, 63488:67712]
    EBUF = [AB[:, 67632:68144], AB[:, 68144:68656]]
    EM = AB[:, 68656:69168]
    SQB = AB[:, 68656:69168]
    VTB = [AB[:, 69168:71280]]
    XI = AFa[:, 0:4096]
    X2 = AFa[:, 4096:8192]
    AVS = AFa[0:64, 8192:8704]
    NMC = AFa[:, 8704:8720]
    TMQ = AFa[:, 4096:4608]
    RL = ROWS[0:1, 3072:3584]
    K2R = ROWS[0:1, 3072:3584]
    MK2 = prm("MK2")
    NMR = prm("NMROW")
    n1 = prm("n1")

    ld(OWT, _dap(WBF["owT"], 0, [[D, 128], [128 * D, 8], [1, 1024]]),
       waits=p.after())
    p.phase()

    av_t = {}
    d_vtb_last = [0]
    qkv_mark = {}
    VTB2 = [VTB[0], VSTG[:, 0:2112]]
    for c in range(NCH):
        ld(XI, _dap(SC["xF"], c * TC, [[T, 128], [128 * T, 8], [1, TC]]),
           waits=p.after("v", "s", "t"))
        dc = p.cnt["d"]
        p.emit("vector", lambda e: e.tensor_copy(VSTG[:, 0:4096], XI),
               waits=[("d", dc), ("o", p.cnt["o"])]
               + av_t.get(c * NH - 1, []), inc="v")
        p.emit("scalar", lambda e: e.activation(ATO[:, 0:4096], XI,
                                                AF.Square),
               waits=[("d", dc)], inc="s")
        rstd = ROWS[0:1, 2048:2560]
        mrs = ROWS[0:1, 2560:3072]
        _ln_stats(H, VSTG[:, 0:4096], ATO[:, 0:4096], rstd, mrs)
        vc, sc = p.cnt["v"], p.cnt["s"]
        p.emit("tensor", lambda e: e.matmul(
            PSA[:, 0:512], ONESR, rstd, start=True, stop=True),
            waits=[("v", vc), ("s", sc)], inc="t")
        p.emit("tensor", lambda e: e.matmul(
            PSA[:, 512:1024], ONESR, mrs, start=True, stop=True), inc="t")
        tc_ = p.cnt["t"]
        for i in range(8):
            p.emit("vector", lambda e, i=i: e.tensor_mul(
                TMQ, XI[:, i * 512:(i + 1) * 512], PSA[:, 0:512]),
                waits=[("t", tc_)], inc="v")
            p.emit("vector", lambda e, i=i: e.tensor_sub(
                XI[:, i * 512:(i + 1) * 512], TMQ, PSA[:, 512:1024]),
                inc="v")
        p.emit("scalar", lambda e: e.activation(ATO[:, 0:4096], XI,
                                                AF.Square),
               waits=[("v", p.cnt["v"])], inc="s")
        _ln_stats(H, None, ATO[:, 0:4096], rstd, rms=True, eps=EPS_RMS)
        sc = p.cnt["s"]
        p.emit("tensor", lambda e: e.matmul(
            PSA[:, 0:512], ONESR, rstd, start=True, stop=True),
            waits=[("s", sc)], inc="t")
        tc_ = p.cnt["t"]
        for i in range(8):
            p.emit("vector", lambda e, i=i: e.tensor_mul(
                TMQ, XI[:, i * 512:(i + 1) * 512], PSA[:, 0:512]),
                waits=[("t", tc_)], inc="v")
            p.emit("vector", lambda e, i=i: e.tensor_scalar_mul(
                LN1[:, i * 512:(i + 1) * 512], TMQ, n1[:, i: i + 1]),
                inc="v")
        vln1 = p.cnt["v"]
        for grp in range(3):
            for kb in range(8):
                src = _dap(WBF["qkvT"], kb * 128 * 3 * D + grp * 1024,
                           [[3 * D, 128], [1, 1024]])
                qw = qkv_mark.get(kb % 2) or p.after("t")
                if kb % 2 == 0:
                    ld(QKVW[0][:, 0:1024], src, waits=qw)
                    dm = ("d", p.cnt["d"])
                else:
                    st(QKVW[1][:, 0:1024], src, waits=qw)
                    dm = ("o", p.cnt["o"])
                for e8 in range(8):
                    bank = (PSA if e8 < 4 else PSB)[
                        :, (e8 % 4) * 512:((e8 % 4) + 1) * 512]
                    p.emit("tensor", lambda e, kb=kb, e8=e8, grp=grp,
                           bk=bank: e.matmul(
                               bk, QKVW[kb % 2][:, e8 * 128:
                                                e8 * 128 + 128],
                               LN1[:, kb * 512:(kb + 1) * 512],
                               start=(kb == 0), stop=(kb == 7)),
                           waits=(([("v", vln1), dm]
                                   + p.after("s")) if kb == 0 else
                                  [dm] if e8 == 0 else ()),
                           inc="t" if e8 == 7 else None)
                qkv_mark[kb % 2] = [("t", p.cnt["t"])]
            tc_ = p.cnt["t"]
            for e8 in range(8):
                bank = (PSA if e8 < 4 else PSB)[
                    :, (e8 % 4) * 512:((e8 % 4) + 1) * 512]
                if grp == 0:
                    dst = QS[:, e8 * 512:(e8 + 1) * 512]
                elif grp == 1:
                    dst = KA[:, e8 * 4096 + c * 512: e8 * 4096 + c * 512
                             + 512]
                else:
                    dst = VSTG[:, e8 * 512:(e8 + 1) * 512]
                if grp <= 1:
                    p.emit("scalar", lambda e, d=dst, bk=bank:
                           e.activation(d, bk, AF.Identity,
                                        scale=0.3535533905932738),
                           waits=[("t", tc_)], inc="s")
                else:
                    gw = [("t", tc_)]
                    if e8 == 0:
                        gw = gw + [("d", d_vtb_last[0])]
                    p.emit("scalar", lambda e, d=dst, bk=bank:
                           e.copy(d, bk), waits=gw, inc="s")
        sqkv = p.cnt["s"]
        p.emit("vector", lambda e: e.memset(VTS, 1.0),
               waits=p.after("o"), inc="v")
        vm = p.cnt["v"]
        for r in range(4):
            for vb in range(8):
                bank = PSA[:, ((r * 8 + vb) % 2) * 512:
                           ((r * 8 + vb) % 2) * 512 + 512]
                ptb = bank.bitcast(BF16)[:, 0:128]
                p.emit("tensor", lambda e, r=r, vb=vb, o=ptb: e.transpose(
                    o, VSTG[:, vb * 512 + r * 128: vb * 512 + r * 128
                            + 128], IDB),
                    waits=[("s", sqkv), ("v", vm)] if (r == 0 and vb == 0)
                    else p.after("s"), inc="t")
                tc_ = p.cnt["t"]
                p.emit("scalar", lambda e, r=r, vb=vb, i=ptb: e.copy(
                    VTS[:, r * 1056 + (2 * vb) * 66:
                        r * 1056 + (2 * vb) * 66 + 64], i[:, 0:64]),
                    waits=[("t", tc_)], inc="s")
                p.emit("scalar", lambda e, r=r, vb=vb, i=ptb: e.copy(
                    VTS[:, r * 1056 + (2 * vb + 1) * 66:
                        r * 1056 + (2 * vb + 1) * 66 + 64],
                    i[:, 64:128]), inc="s")
        scv = p.cnt["s"]
        t_vt = p.cnt["t"]
        for r in range(4):
            kvb = c * 4 + r
            st(_dap(SC["VTd"], kvb * 16 * 128 * 66,
                    [[66, 128], [128 * 66, 16], [1, 66]]),
               bass.AP(VTS.tensor, VTS.offset + r * 1056,
                       [[AB_N, 128], [66, 16], [1, 66]]),
               waits=[("s", scv)])
        MQROW = ROWS[0:1, 3584:3600]
        MXROW = ROWS[0:1, 3568:3584]
        mq_marks = {}
        for h in range(NH + 2):
            if h < NH:
                par = h % 2
                b0 = 64 * par
                qsl = QS[b0:b0 + 64, (h // 2) * 512:(h // 2) * 512 + 512]
                ksl = KA[b0:b0 + 64, (h // 2) * 4096 + c * 512:
                         (h // 2) * 4096 + c * 512 + 512]
                sqsl = SQB[b0:b0 + 64, 0:512]
                onesl = ONESCB[b0:b0 + 64, :]
                p.emit("scalar", lambda e, k=ksl, s=sqsl: e.activation(
                    s, k, AF.Square), waits=mq_marks.get(h - 2, []),
                    inc="s")
                sc = p.cnt["s"]
                p.emit("tensor", lambda e, s=sqsl, o=onesl: e.matmul(
                    PSB[0:1, 1536:2048], o, s, start=True, stop=True),
                    waits=[("s", sc)], inc="t")
                tc_ = p.cnt["t"]
                p.emit("scalar", lambda e: e.copy(K2R,
                                                  PSB[0:1, 1536:2048]),
                       waits=[("t", tc_)], inc="s")
                sc = p.cnt["s"]
                p.emit("vector", lambda e, h=h: e.tensor_reduce(
                    MXROW[0:1, h:h + 1], K2R, AX.X, ALU.max),
                    waits=[("s", sc)], inc="v")
                p.emit("scalar", lambda e, q=qsl, s=sqsl: e.activation(
                    s, q, AF.Square), waits=p.after("v"), inc="s")
                sc = p.cnt["s"]
                p.emit("tensor", lambda e, s=sqsl, o=onesl: e.matmul(
                    PSB[0:1, 1024:1536], o, s, start=True, stop=True),
                    waits=[("s", sc)], inc="t")
                tc_ = p.cnt["t"]
                mq_marks[h] = [("t", tc_)]
                p.emit("vector", lambda e, h=h: e.reduce_max(
                    MQROW[0:1, h:h + 1], PSB[0:1, 1024:1536], axis=AX.X),
                    waits=[("t", tc_)], inc="v")
            if NH > h >= 1:
                p.emit("vector", lambda e, h=h - 1: e.tensor_max(
                    MK2[0:1, h:h + 1], MK2[0:1, h:h + 1],
                    MXROW[0:1, h:h + 1]), inc="v")
            if NH + 1 > h >= 2:
                p.emit("vector", lambda e, h=h - 2: e.tensor_add(
                    NMR[0:1, h:h + 1], MQROW[0:1, h:h + 1],
                    MK2[0:1, h:h + 1]), inc="v")
            if h >= 3:
                p.emit("vector", lambda e, h=h - 3: e.tensor_single_scalar(
                    NMR[0:1, h:h + 1], NMR[0:1, h:h + 1], -1.0 / 16.0,
                    ALU.mult), inc="v")
        for h in (NH - 1,):
            p.emit("vector", lambda e, h=h: e.tensor_max(
                MK2[0:1, h:h + 1], MK2[0:1, h:h + 1], MXROW[0:1, h:h + 1]),
                inc="v")
        p.emit("vector", lambda e: e.tensor_copy(
            ROWS[0:1, 3560:3561], ROWS[0:1, 3561:3562]), inc="v")
        for h in (NH - 1,):
            p.emit("vector", lambda e, h=h: e.tensor_add(
                NMR[0:1, h:h + 1], MQROW[0:1, h:h + 1], MK2[0:1, h:h + 1]),
                inc="v")
        p.emit("vector", lambda e: e.tensor_copy(
            ROWS[0:1, 3560:3561], ROWS[0:1, 3561:3562]), inc="v")
        for h in (NH - 1,):
            p.emit("vector", lambda e, h=h: e.tensor_single_scalar(
                NMR[0:1, h:h + 1], NMR[0:1, h:h + 1], -0.5,
                ALU.mult), inc="v")
        vnm = p.cnt["v"]
        p.emit("tensor", lambda e: e.matmul(
            PSB[:, 1024:1040], ONESR, NMR[0:1, 0:16], start=True,
            stop=True), waits=[("v", vnm)], inc="t")
        tc_ = p.cnt["t"]
        p.emit("vector", lambda e: e.tensor_copy(NMC, PSB[:, 1024:1040]),
               waits=[("t", tc_)], inc="v")
        vnmc = p.cnt["v"]
        nkvb = 4 * (c + 1)
        oc0 = p.cnt["o"]
        last_e = {}
        mask_mark = {}
        pend = [None]
        for h in range(NH):
            par = h % 2
            b0 = 64 * par
            vtb = VTB2[par]
            ldw = [("o", oc0)] + av_t.get(c * NH + h - 2, [])
            if h == 1:
                ldw = [("o", oc0), ("t", t_vt)]
            ld(vtb[:, 0:nkvb * 66],
               _dap(SC["VTd"], h * 128 * 66,
                    [[66, 128], [16 * 128 * 66, nkvb], [1, 66]]),
               waits=ldw)
            dch = p.cnt["d"]
            if h == NH - 1:
                d_vtb_last[0] = p.cnt["d"]
            avp = (PSB[0:65, 1536:2048] if par == 0
                   else PSB[0:65, 1024:1536])
            t_sc = {}
            t_av = {}

            def emit_sc(kvb, h=h, b0=b0):
                bi = (0, 1, 3)[kvb % 3]
                bank = PSA[:, bi * 512: bi * 512 + 512]
                p.emit("tensor", lambda e, h=h, kvb=kvb, bk=bank,
                       b0=b0: e.matmul(
                           bk, KA[b0:b0 + 64, (h // 2) * 4096 + kvb
                                  * 128: (h // 2) * 4096 + kvb * 128
                                  + 128],
                           QS[b0:b0 + 64, (h // 2) * 512:
                              (h // 2) * 512 + 512],
                           start=True, stop=True),
                       waits=last_e.get(bi, [("s", scv)]), inc="t")
                t_sc[kvb] = [("t", p.cnt["t"])]
                return bank

            banks = {}
            for kvb in range(min(2, nkvb)):
                banks[kvb] = emit_sc(kvb)
            if c == 0 and pend[0] is not None:
                pend[0]()
                pend[0] = None
            for kvb in range(nkvb):
                bi = (0, 1, 3)[kvb % 3]
                bank = banks.pop(kvb)
                esl = EBUF[kvb % 2]
                ew = t_sc[kvb] + mask_mark.get(kvb % 2, [])
                if h == 0 and kvb == 0:
                    ew = ew + [("v", vnmc)]
                p.emit("scalar", lambda e, bk=bank, o=esl, h=h:
                       e.activation(o, bk, AF.Exp, bias=NMC[:, h:h + 1]),
                       waits=ew, inc="s")
                se = p.cnt["s"]
                last_e[bi] = [("s", se)]
                rhs = esl
                if kvb >= c * 4:
                    r = kvb - c * 4
                    p.emit("vector", lambda e, i=esl, r=r:
                           e.tensor_mul(EM, i,
                                        CMASK[:, (3 - r) * 128:(3 - r) * 128 + 512]),
                           waits=[("s", se)] + t_av.get(kvb - 1, [])
                           + p.after("o"), inc="v")
                    rhs = EM
                    mask_mark[kvb % 2] = [("v", p.cnt["v"])]
                p.emit("tensor", lambda e, kvb=kvb, rhs=rhs, vtb=vtb,
                       av=avp, nk=nkvb: e.matmul(
                           av, vtb[:, kvb * 66: kvb * 66 + 65], rhs,
                           start=(kvb == 0), stop=(kvb == nk - 1)),
                       waits=[("s", p.cnt["s"]), ("v", p.cnt["v"]),
                              ("d", dch)], inc="t")
                t_av[kvb] = [("t", p.cnt["t"])]
                if kvb == 0 and pend[0] is not None:
                    pend[0]()
                    pend[0] = None
                if kvb + 2 < nkvb:
                    banks[kvb + 2] = emit_sc(kvb + 2)
            tav = p.cnt["t"]
            av_t[c * NH + h] = [("t", tav)]
            p.emit("vector", lambda e, av=avp: e.reciprocal(
                RL, av[64:65, 0:512]), waits=[("t", tav)], inc="v")
            vrc = p.cnt["v"]

            def tail(h=h, par=par, avp=avp, vrc=vrc):
                p.emit("tensor", lambda e: e.matmul(
                    PSA[0:64, 1024:1536], ONESR[0:1, 0:64], RL,
                    start=True, stop=True), waits=[("v", vrc)], inc="t")
                tc_ = p.cnt["t"]
                p.emit("scalar", lambda e, av=avp: e.copy(
                    AVS, av[0:64, 0:512]), waits=[("t", tc_)], inc="s")
                sc = p.cnt["s"]
                if par == 0:
                    dst = ATO[0:64, (h // 2) * 512:(h // 2) * 512 + 512]
                    p.emit("vector", lambda e, d=dst: e.tensor_mul(
                        d, AVS, PSA[0:64, 1024:1536]),
                        waits=[("s", sc)], inc="v")
                else:
                    stg = EM[0:64, 0:512]
                    p.emit("vector", lambda e, d=stg: e.tensor_mul(
                        d, AVS, PSA[0:64, 1024:1536]),
                        waits=[("s", sc)] + p.after("o"), inc="v")
                    vc = p.cnt["v"]
                    st(bass.AP(ATO.tensor, ATO.offset + 64 * AB_N
                               + (h // 2) * 512,
                               [[AB_N, 64], [1, 512]]),
                       bass.AP(EM.tensor, EM.offset,
                               [[AB_N, 64], [1, 512]]),
                       waits=[("v", vc)])
            pend[0] = tail
        pend[0]()
        pend[0] = None
        oc = p.cnt["o"]
        vato = p.cnt["v"]
        for db in range(8):
            bank = PSA[:, (db % 4) * 512:((db % 4) + 1) * 512]
            for kb in range(8):
                p.emit("tensor", lambda e, db=db, kb=kb, bk=bank: e.matmul(
                    bk, OWT[:, kb * 1024 + db * 128: kb * 1024 + db * 128
                            + 128], ATO[:, kb * 512:(kb + 1) * 512],
                    start=(kb == 0), stop=(kb == 7)),
                    waits=(([("o", oc), ("v", vato)] + p.after("s"))
                           if kb == 0 else ()),
                    inc="t" if kb == 7 else None)
            tc_ = p.cnt["t"]
            p.emit("vector", lambda e, db=db, bk=bank: e.tensor_add(
                X2[:, db * 512:(db + 1) * 512],
                XI[:, db * 512:(db + 1) * 512], bk),
                waits=[("t", tc_)], inc="v")
        vc = p.cnt["v"]
        st(_dap(SC["x2F"], c * TC, [[T, 128], [128 * T, 8], [1, TC]]),
           X2, waits=[("v", vc)])
        p.phase()


def _moe(H):
    p, AB, AFa = H["p"], H["AB"], H["AFa"]
    ROWS, PSA, PSB = H["ROWS"], H["PSA"], H["PSB"]
    IN, SC, WBF = H["IN"], H["SC"], H["WBF"]
    ld, st, prm = H["ld"], H["st"], H["prm"]
    IDF, IDB, ONESR, ONESCB = H["IDF"], H["IDB"], H["ONESR"], H["ONESCB"]

    DWN = AB[:, 0:32768]
    UPW = AB[:, 32768:49152]
    UPO = AB[:, 49152:65536]
    LN2 = AB[:, 65536:69632]
    GS = AB[0:8, 69632:70144]
    FRSC = AB[:, 70144:70656]
    GATET = AB[:, 70656:70720]
    SQM = LN2
    XC2 = AFa[:, 0:4096]
    TMQ = AFa[:, 4096:4608]
    GPT = AFa[:, 4608:4640]
    EG = AFa[:, 4640:4672]
    EMK = AFa[:, 4672:4704]
    COLS = AFa[:, 4704:4770]
    X3 = AFa[:, 4772:8868]
    GLD = AFa[:, 0:64]

    ld(DWN, _dap(WBF["downT"], 0, [[D, 128], [128 * D, 32], [1, 1024]]),
       waits=p.after())
    ld(GLD, _dap(IN["gateT"], 0, [[NEX, 128], [128 * NEX, 8], [1, NEX]]))
    dc = p.cnt["d"]
    p.emit("vector", lambda e: e.tensor_copy(GATET, GLD),
           waits=[("d", dc)], inc="v")
    n2 = prm("n2")
    p.phase()

    xc2_free = p.after("v", "s", "t")
    upw_free = p.after("t")
    x3st = []
    upsil = {}
    for c in range(NCH):
        ld(XC2, _dap(SC["x2F"], c * TC, [[T, 128], [128 * T, 8],
                                         [1, TC]]),
           waits=xc2_free)
        dc = p.cnt["d"]
        ld(UPW, _dap(WBF["upT"], 0,
                     [[ED, 128], [128 * ED, 8], [1, 2048]]),
           waits=upw_free)
        dup = p.cnt["d"]
        p.emit("scalar", lambda e: e.activation(SQM, XC2, AF.Square),
               waits=[("d", dc), ("o", p.cnt["o"])] + upw_free, inc="s")
        rstd = ROWS[0:1, 2048:2560]
        _ln_stats(H, None, SQM, rstd, rms=True, eps=EPS_RMS)
        sc = p.cnt["s"]
        p.emit("tensor", lambda e: e.matmul(
            PSA[:, 0:512], ONESR, rstd, start=True, stop=True),
            waits=[("s", sc)], inc="t")
        tc_ = p.cnt["t"]
        for i in range(8):
            p.emit("vector", lambda e, i=i: e.tensor_mul(
                TMQ, XC2[:, i * 512:(i + 1) * 512], PSA[:, 0:512]),
                waits=[("t", tc_)], inc="v")
            p.emit("vector", lambda e, i=i: e.tensor_scalar_mul(
                LN2[:, i * 512:(i + 1) * 512], TMQ, n2[:, i: i + 1]),
                inc="v")
        vln = p.cnt["v"]
        for kb in range(8):
            p.emit("tensor", lambda e, kb=kb: e.matmul(
                PSB[0:8, 1024:1536], GATET[:, kb * 8:(kb + 1) * 8],
                LN2[:, kb * 512:(kb + 1) * 512],
                start=(kb == 0), stop=(kb == 7)),
                waits=[("v", vln)] if kb == 0 else (),
                inc="t" if kb == 7 else None)
        tc_ = p.cnt["t"]
        p.emit("scalar", lambda e: e.copy(GS, PSB[0:8, 1024:1536]),
               waits=[("t", tc_)], inc="s")
        sc = p.cnt["s"]
        gptm = p.after("v")
        for r in range(4):
            ptb = PSA[:, 1024:1536].bitcast(BF16)[0:128, 0:8]
            p.emit("tensor", lambda e, r=r, o=ptb: e.transpose(
                o, GS[:, r * 128:(r + 1) * 128], IDB[0:8, 0:8]),
                waits=([("s", sc)] if r == 0 else []) + gptm, inc="t")
            tc_ = p.cnt["t"]
            p.emit("vector", lambda e, r=r, i=ptb: e.tensor_copy(
                GPT[:, r * 8:(r + 1) * 8], i), waits=[("t", tc_)],
                inc="v")
            gptm = [("v", p.cnt["v"])]
            vc = p.cnt["v"]
            mx = COLS[:, r: r + 1]
            p.emit("vector", lambda e, r=r, mx=mx: e.reduce_max(
                mx, GPT[:, r * 8:(r + 1) * 8], axis=AX.X),
                waits=[("t", p.cnt["t"])], inc="v")
            vc = p.cnt["v"]
            p.emit("gpsimd", lambda e, r=r, mx=mx: e.tensor_single_scalar(
                EG[:, r * 8:(r + 1) * 8], GPT[:, r * 8:(r + 1) * 8],
                mx, ALU.subtract), waits=[("v", vc)], inc="g")
            gc = p.cnt["g"]
            p.emit("scalar", lambda e, r=r: e.activation(
                EG[:, r * 8:(r + 1) * 8], EG[:, r * 8:(r + 1) * 8],
                AF.Exp), waits=[("g", gc)], inc="s")
            sc2 = p.cnt["s"]
            tsum = COLS[:, 4 + r: 5 + r]
            m1 = COLS[:, 8 + r: 9 + r]
            m2c = COLS[:, 12 + r: 13 + r]
            p.emit("vector", lambda e, t=tsum, r=r: e.reduce_sum(
                t, EG[:, r * 8:(r + 1) * 8], axis=AX.X),
                waits=[("s", sc2)], inc="v")
            p.emit("vector", lambda e, m=m1, r=r: e.reduce_max(
                m, EG[:, r * 8:(r + 1) * 8], axis=AX.X), inc="v")
            vc = p.cnt["v"]
            p.emit("gpsimd", lambda e, r=r, m=m1: e.tensor_scalar(
                EMK[:, r * 8:(r + 1) * 8], EG[:, r * 8:(r + 1) * 8],
                m, None, ALU.is_ge), waits=[("v", vc)], inc="g")
            p.emit("gpsimd", lambda e, r=r: e.tensor_scalar(
                EMK[:, r * 8:(r + 1) * 8], EMK[:, r * 8:(r + 1) * 8],
                -1.0, 1.0, ALU.mult, ALU.add), inc="g")
            p.emit("gpsimd", lambda e, r=r: e.tensor_mul(
                EMK[:, r * 8:(r + 1) * 8], EMK[:, r * 8:(r + 1) * 8],
                EG[:, r * 8:(r + 1) * 8]), inc="g")
            gc = p.cnt["g"]
            p.emit("vector", lambda e, m=m2c, r=r: e.reduce_max(
                m, EMK[:, r * 8:(r + 1) * 8], axis=AX.X),
                waits=[("g", gc)], inc="v")
            s12 = COLS[:, 16 + r: 17 + r]
            den = COLS[:, 20 + r: 21 + r]
            rden = COLS[:, 24 + r: 25 + r]
            fac = COLS[:, 28 + r: 29 + r]
            spc = COLS[:, 32:33]
            p.emit("vector", lambda e, s=spc: e.tensor_copy(s, s), inc="v")
            p.emit("vector", lambda e, m=m1, m2=m2c, s=s12:
                   e.tensor_add(s, m, m2), inc="v")
            p.emit("vector", lambda e, s=spc: e.tensor_copy(s, s), inc="v")
            p.emit("vector", lambda e, t=tsum, s=s12, d=den:
                   e.scalar_tensor_tensor(d, t, 1e-9, s, ALU.mult,
                                          ALU.add), inc="v")
            p.emit("vector", lambda e, s=spc: e.tensor_copy(s, s), inc="v")
            p.emit("vector", lambda e, d=den, rd=rden: e.reciprocal(
                rd, d), inc="v")
            p.emit("vector", lambda e, s=spc: e.tensor_copy(s, s), inc="v")
            p.emit("vector", lambda e, s=s12, rd=rden, f=fac:
                   e.tensor_mul(f, s, rd), inc="v")
            vc = p.cnt["v"]
            ptf = PSA[:, 1536:1664]
            p.emit("tensor", lambda e, f=fac, o=ptf: e.transpose(
                o[0:1, 0:128], f, IDF), waits=[("v", vc)], inc="t")
            tc_ = p.cnt["t"]
            p.emit("scalar", lambda e, r=r, i=ptf: e.copy(
                ROWS[0:1, 3072 + r * 128:3072 + (r + 1) * 128], i[0:1, 0:128]),
                waits=[("t", tc_)], inc="s")
        sgate = p.cnt["s"]
        for ebh in range(2):
            if ebh == 1:
                ld(UPW, _dap(WBF["upT"], 2048,
                             [[ED, 128], [128 * ED, 8], [1, 2048]]),
                   waits=[("t", p.cnt["t"])])
                dup = p.cnt["d"]
            for eb in range(16):
                bank = PSB[:, (eb % 2) * 512:((eb % 2) + 1) * 512]
                for kb in range(8):
                    p.emit("tensor", lambda e, eb=eb, kb=kb, bk=bank:
                           e.matmul(bk, UPW[:, kb * 2048 + eb * 128:
                                            kb * 2048 + eb * 128 + 128],
                                    LN2[:, kb * 512:(kb + 1) * 512],
                                    start=(kb == 0), stop=(kb == 7)),
                           waits=(([("d", dup)]
                                   + upsil.get(eb % 2, p.after("s")))
                                  if kb == 0 else ()),
                           inc="t" if kb == 7 else None)
                tc_ = p.cnt["t"]
                p.emit("scalar", lambda e, ebh=ebh, eb=eb, bk=bank:
                       e.activation(UPO[:, (ebh * 16 + eb) * 512:
                                        (ebh * 16 + eb) * 512 + 512],
                                    bk, AF.Silu),
                       waits=[("t", tc_)], inc="s")
                upsil[eb % 2] = [("s", p.cnt["s"])]
        upw_free = [("t", p.cnt["t"])]
        p.emit("tensor", lambda e: e.matmul(
            PSA[:, 1024:1536], ONESR, ROWS[0:1, 3072:3584], start=True,
            stop=True), waits=[("s", sgate)] + gptm, inc="t")
        tc_ = p.cnt["t"]
        p.emit("scalar", lambda e: e.copy(FRSC, PSA[:, 1024:1536]),
               waits=[("t", tc_)] + (xc2_free if c > 0 else []),
               inc="s")
        sfr = p.cnt["s"]
        sup = p.cnt["s"]
        for db in range(8):
            bank = PSA[:, (db % 4) * 512:((db % 4) + 1) * 512]
            for kb in range(32):
                p.emit("tensor", lambda e, db=db, kb=kb, bk=bank: e.matmul(
                    bk, DWN[:, kb * 1024 + db * 128: kb * 1024 + db * 128
                            + 128], UPO[:, kb * 512:(kb + 1) * 512],
                    start=(kb == 0), stop=(kb == 31)),
                    waits=(([("s", sup)] + p.after("v")) if kb == 0
                           else ()),
                    inc="t" if kb == 31 else None)
            tc_ = p.cnt["t"]
            p.emit("vector", lambda e, db=db, bk=bank: e.tensor_mul(
                TMQ, bk, FRSC), waits=[("t", tc_), ("s", sfr)], inc="v")
            p.emit("vector", lambda e, db=db: e.tensor_add(
                X3[:, db * 512:(db + 1) * 512],
                XC2[:, db * 512:(db + 1) * 512], TMQ),
                waits=x3st if db == 0 else (), inc="v")
        vc = p.cnt["v"]
        xc2_free = [("v", vc)]
        st(_dap(SC["x2F"], c * TC, [[T, 128], [128 * T, 8], [1, TC]]),
           X3, waits=[("v", vc)])
        x3st = [("o", p.cnt["o"])]
    p.phase()


def _exit(H, OUT):
    p, AFa, PSA = H["p"], H["AFa"], H["PSA"]
    SC, ld, st, IDF = H["SC"], H["ld"], H["st"], H["IDF"]
    A1 = AFa[:, 0:4096]
    A3 = AFa[:, 4096:8192]
    OST = AFa[:, 4096:5120]
    OST2 = AFa[:, 5120:6144]

    ost_mark = {}
    st_mark = {}
    for c in range(NCH):
        ld(A1, _dap(SC["xF"], c * TC, [[T, 128], [128 * T, 8], [1, TC]]),
           waits=p.after("v", "s", "t", "o"))
        ld(A3, _dap(SC["x2F"], c * TC, [[T, 128], [128 * T, 8],
                                        [1, TC]]))
        dc = p.cnt["d"]
        for i in range(8):
            p.emit("vector", lambda e, i=i: e.tensor_add(
                A1[:, i * 512:(i + 1) * 512],
                A1[:, i * 512:(i + 1) * 512],
                A3[:, i * 512:(i + 1) * 512]),
                waits=[("d", dc), ("o", p.cnt["o"])], inc="v")
        vc = p.cnt["v"]
        for r in range(4):
            ob = (OST, OST2)[r % 2]
            for i in range(8):
                k = (r * 8 + i) % 4
                bank = PSA[:, k * 512: k * 512 + 128]
                p.emit("tensor", lambda e, r=r, i=i, o=bank: e.transpose(
                    o, A1[:, i * 512 + r * 128: i * 512 + r * 128 + 128],
                    IDF),
                    waits=[("v", vc)] + ost_mark.get(k, p.after("s")),
                    inc="t")
                tc_ = p.cnt["t"]
                p.emit("scalar", lambda e, i=i, o=bank, ob=ob: e.copy(
                    ob[:, i * 128:(i + 1) * 128], o),
                    waits=[("t", tc_)]
                    + (st_mark.get(r % 2, p.after("o")) if i == 0
                       else []), inc="s")
                ost_mark[k] = [("s", p.cnt["s"])]
            sc = p.cnt["s"]
            st(_dap(OUT, (c * TC + r * 128) * D, [[D, 128], [1, D]]),
               ob[:, 0:1024], waits=[("s", sc)])
            st_mark[r % 2] = [("o", p.cnt["o"])]


def kernel(**inputs):
    consts = _prep_consts()
    nc = build()
    in_maps = []
    for b in range(8):
        m = _prep_core(inputs, b)
        m.update(consts)
        in_maps.append(m)
    res = run_bass_kernel_spmd(nc, in_maps, core_ids=list(range(8)))
    out = np.stack([np.asarray(r["out"]) for r in res.results], axis=0)
    return out.astype(np.float32)


# version 15
# speedup vs baseline: 1.3711x; 1.0083x over previous
import numpy as np
import ml_dtypes
import concourse.bass as bass
import concourse.mybir as mybir
from concourse.bass_utils import run_bass_kernel_spmd

F32 = mybir.dt.float32
BF16 = mybir.dt.bfloat16
AF = mybir.ActivationFunctionType
ALU = mybir.AluOpType
AX = mybir.AxisListType

T = 4096
D = 1024
E = 2048
NST = 32
RNK = 64
NJ = 16
TC = 512
NCH = 8
HF = 2048
NH = 16
ED = 4096
NEX = 8
BF = ml_dtypes.bfloat16
EPS_LN = 1e-5
EPS_RMS = 1e-6

AB_N = 71728
AF_N = 8868
AB_TAIL = 67632

_FVALS = [1e-5, 1e-6, 1.0, -0.5, 1.0 / 1024.0, 0.3535533905932738]


class Plan:
    ENGS = ("sync", "vector", "scalar", "tensor", "gpsimd")

    def __init__(self):
        self.ev = {e: [] for e in self.ENGS}
        self.cnt = {}
        self.owner = {}
        self._pend = {e: [] for e in self.ENGS}

    def sem_def(self, name, owner):
        self.cnt[name] = 0
        self.owner[name] = owner

    def after(self, *names):
        if not names:
            names = tuple(self.cnt.keys())
        return [(n, self.cnt[n]) for n in names if self.cnt[n] > 0]

    def emit(self, eng, fn, waits=(), inc=None, inc_by=1):
        w = self._pend[eng] + list(waits)
        self._pend[eng] = []
        incs = []
        if inc is not None:
            self.cnt[inc] += inc_by
            incs = [(inc, inc_by)]
        self.ev[eng].append((fn, w, incs))

    def phase(self):
        m = self.after()
        for e in self.ENGS:
            self._pend[e] = list(m)


def _replay(plan, sems, engname):
    done = {}

    def _(eng):
        for fn, waits, incs in plan.ev[engname]:
            best = {}
            for n, v in waits:
                if plan.owner[n] == engname:
                    continue
                if done.get(n, -1) >= v:
                    continue
                if best.get(n, -1) < v:
                    best[n] = v
            for n, v in best.items():
                eng.wait_ge(sems[n], v)
                done[n] = v
            inst = fn(eng)
            for n, by in incs:
                assert inst is not None, "inc on op returning None"
                inst.then_inc(sems[n], by)
    return _


def _pp(a):
    return np.ascontiguousarray(a, dtype=np.float32)


def _dvec(a):
    a = np.asarray(a)
    n = a.shape[0] // 128
    return np.ascontiguousarray(a.reshape(n, 128).T, dtype=np.float32)


def _prep_consts():
    c = {}
    ident = np.eye(128, dtype=np.float32)
    c["IDF32"] = ident
    selq = np.zeros((128, 512), np.float32)
    for dq in range(4):
        for m in range(128):
            selq[dq * 32 + m % 32, dq * 128 + m] = 1.0
    c["SELQ32"] = selq
    selbc = np.zeros((64, 2048), np.float32)
    for ng in range(8):
        for m in range(128):
            selbc[ng * 4 + m // 32, (ng * 2 + 0) * 128 + m] = 1.0
            selbc[32 + ng * 4 + m // 32, (ng * 2 + 1) * 128 + m] = 1.0
    nsum = np.zeros((128, 32), np.float32)
    for k in range(128):
        nsum[k, k % 32] = 1.0
    cm = np.zeros((128, 896), np.float32)
    for p in range(128):
        cm[p, :] = (np.arange(896) >= p + 384)
    c["ONESR"] = np.ones((1, 128), np.float32)
    fv = np.zeros(8, np.float32)
    fv[:len(_FVALS)] = _FVALS
    c["FCONST"] = np.broadcast_to(fv, (128, 8)).copy()
    c["IDBF_f32"] = ident.copy()
    c["SELQBF_f32"] = selq.copy()
    c["SELBC_f32"] = selbc
    c["NSUM_f32"] = nsum
    c["CMASK_f32"] = cm
    c["ONESCB_f32"] = np.ones((128, 1), np.float32)
    return c


def _prep_core(inputs, b):
    g = inputs
    d = {"x": np.ascontiguousarray(np.asarray(g["L0_attn_x"])[b],
                                   dtype=np.float32)}
    for dirn, sfx in (("f", "_f"), ("b", "_b")):
        d[f"WinT_{dirn}"] = _pp(np.asarray(g["L0_attn_Win" + sfx]).T)
        d[f"WxT_{dirn}"] = _pp(np.asarray(g["L0_attn_Wx" + sfx]).T)
        d[f"WdtT_{dirn}"] = _pp(np.asarray(g["L0_attn_Wdt" + sfx]).T)
        d[f"WoutT_{dirn}"] = _pp(np.asarray(g["L0_attn_Wout" + sfx]).T)
        d[f"convw_{dirn}"] = np.ascontiguousarray(
            np.asarray(g["L0_attn_convw" + sfx]).reshape(16, 128, 4)
            .transpose(1, 0, 2), dtype=np.float32)
        d[f"convb_{dirn}"] = _dvec(g["L0_attn_convb" + sfx])
        d[f"bdt_{dirn}"] = _dvec(g["L0_attn_bdt" + sfx])
        d[f"Dp_{dirn}"] = _dvec(g["L0_attn_Dp" + sfx])
        d[f"lng_{dirn}"] = _dvec(g["L0_attn_ln_g" + sfx])
        d[f"lnb_{dirn}"] = _dvec(g["L0_attn_ln_b" + sfx])
        al = np.asarray(g["L0_attn_Alog" + sfx]).reshape(16, 4, 32, 8, 4)
        d[f"AlogL_{dirn}"] = np.ascontiguousarray(
            al.transpose(4, 2, 0, 1, 3).reshape(128, 512), dtype=np.float32)
    d["WfuseT"] = _pp(np.asarray(g["L0_attn_Wfuse"]).T)
    d["bfuse"] = _dvec(g["L0_attn_bfuse"])
    d["qkvT"] = _pp(np.asarray(g["L0_ffn_qkv_w"]).T)
    d["owT"] = _pp(np.asarray(g["L0_ffn_o_w"]).T)
    d["n1"] = _dvec(g["L0_ffn_n1_w"])
    d["n2"] = _dvec(g["L0_ffn_n2_w"])
    d["gateT"] = _pp(np.asarray(g["L0_ffn_gate_w"]).T)
    d["upT"] = _pp(np.asarray(g["L0_ffn_up_w"]).T)
    d["downT"] = _pp(np.asarray(g["L0_ffn_down_w"]).T)
    return d


def build(dbg=(), dirs=("f", "b"), phases=("mamba", "attn", "moe", "exit"), halves=(0, 1), p1c=True, lite1=False, kadump=False):
    nc = bass.Bass("TRN2", target_bir_lowering=False)
    IN = {}

    def di(name, shape):
        IN[name] = nc.dram_tensor(name, list(shape), F32,
                                  kind="ExternalInput").ap()

    di("x", (T, D))
    for dirn in ("f", "b"):
        di(f"WinT_{dirn}", (D, 2 * E))
        di(f"WxT_{dirn}", (E, 128))
        di(f"WdtT_{dirn}", (RNK, E))
        di(f"WoutT_{dirn}", (E, D))
        di(f"convw_{dirn}", (128, 16, 4))
        di(f"convb_{dirn}", (128, 16))
        di(f"bdt_{dirn}", (128, 16))
        di(f"Dp_{dirn}", (128, 16))
        di(f"lng_{dirn}", (128, 8))
        di(f"lnb_{dirn}", (128, 8))
        di(f"AlogL_{dirn}", (128, 512))
    di("WfuseT", (2 * D, D))
    di("bfuse", (128, 8))
    di("qkvT", (D, 3 * D))
    di("owT", (D, D))
    di("n1", (128, 8))
    di("n2", (128, 8))
    di("gateT", (D, NEX))
    di("upT", (D, ED))
    di("downT", (ED, D))
    di("IDF32", (128, 128))
    di("SELQ32", (128, 512))
    di("ONESR", (1, 128))
    di("FCONST", (128, 8))
    di("IDBF_f32", (128, 128))
    di("SELQBF_f32", (128, 512))
    di("SELBC_f32", (64, 2048))
    di("NSUM_f32", (128, 32))
    di("CMASK_f32", (128, 896))
    di("ONESCB_f32", (128, 1))

    OUT = nc.dram_tensor("out", [T, D], F32, kind="ExternalOutput").ap()

    SC = {
        "xF": nc.dram_tensor("xF", [D, T], F32).ap(),
        "x2F": nc.dram_tensor("x2F", [D, T], F32).ap(),
        "ln0F": nc.dram_tensor("ln0F", [D, T], BF16).ap(),
        "ucF": nc.dram_tensor("ucF", [E, T], BF16).ap(),
        "zsF": nc.dram_tensor("zsF", [E, T], BF16).ap(),
        "ygF": nc.dram_tensor("ygF", [E, T], BF16).ap(),
        "foF": nc.dram_tensor("foF", [D, T], BF16).ap(),
        "VTd": nc.dram_tensor("VTd", [32 * NH * 128, 66], BF16).ap(),
    }
    WBF = {}
    for nm, shape in [("WinT_f", (D, 2 * E)), ("WinT_b", (D, 2 * E)),
                      ("WoutT_f", (E, D)), ("WoutT_b", (E, D)),
                      ("WfuseT", (2 * D, D)), ("qkvT", (D, 3 * D)),
                      ("owT", (D, D)), ("upT", (D, ED)),
                      ("downT", (ED, D))]:
        WBF[nm] = nc.dram_tensor("bf_" + nm, list(shape), BF16).ap()

    DBG = {}
    for name, key in dbg:
        src = SC[key] if key in SC else WBF[key]
        DBG[name] = (nc.dram_tensor("dbg_" + name, list(src.shape),
                                    src.dtype, kind="ExternalOutput").ap(),
                     src)

    p = Plan()
    for s, o in [("d", "sync"), ("o", "gpsimd"), ("t", "tensor"),
                 ("v", "vector"), ("s", "scalar"), ("g", "gpsimd")]:
        p.sem_def(s, o)

    with (
        nc.sbuf_tensor("ABm", [128, AB_N], BF16) as ABm,
        nc.sbuf_tensor("AFm", [128, AF_N], F32) as AFm,
        nc.sbuf_tensor("CONST", [128, 780], F32) as CONST,
        nc.sbuf_tensor("CONSTB", [128, 3620], BF16) as CONSTB,
        nc.sbuf_tensor("PRM", [128, 1344], F32) as PRM,
        nc.sbuf_tensor("ROWS", [128, 3600], F32) as ROWS,
        nc.sbuf_tensor("CAR", [128, 561], F32) as CAR,
        nc.psum_tensor("PSA", [128, 2048], F32) as PSA,
        nc.psum_tensor("PSB", [128, 2048], F32) as PSB,
        nc.semaphore("sem_d") as sd,
        nc.semaphore("sem_o") as so,
        nc.semaphore("sem_t") as st_,
        nc.semaphore("sem_v") as sv,
        nc.semaphore("sem_s") as ss,
        nc.semaphore("sem_g") as sg,
        nc.Block() as block,
    ):
        sems = {"d": sd, "o": so, "t": st_, "v": sv, "s": ss, "g": sg}
        for k, val in enumerate(_FVALS):
            nc.const_aps.register(F32, val, CONST[:, 769 + k: 770 + k])
        _emit_all(nc, p, dict(AB=ABm, AFa=AFm, CONST=CONST, CONSTB=CONSTB,
                              PRM=PRM, ROWS=ROWS, CAR=CAR, PSA=PSA,
                              PSB=PSB),
                  IN, OUT, DBG, SC, WBF, dirs, phases, halves, p1c, lite1, kadump)
        for e in Plan.ENGS:
            getattr(block, e)(_replay(p, sems, e))
    return nc


def _dap(src_ap, offset, dims):
    return bass.AP(src_ap.tensor, src_ap.offset + offset, dims)


def _emit_all(nc, p, A, IN, OUT, DBG, SC, WBF, dirs, phases, halves, p1c, lite1, kadump):
    AB, AFa = A["AB"], A["AFa"]
    CONST, CONSTB, PRM = A["CONST"], A["CONSTB"], A["PRM"]
    ROWS, CAR, PSA, PSB = A["ROWS"], A["CAR"], A["PSA"], A["PSB"]

    IDF = CONST[:, 0:128]
    SELQ = CONST[:, 128:640]
    ONESR = CONST[0:1, 640:768]
    IDB = CONSTB[:, 0:128]
    SELQB = CONSTB[:, 128:640]
    SELBC = CONSTB[64:128, 640:2688]
    NSUM = CONSTB[:, 2688:2720]
    CMASK = CONSTB[:, 2720:3616]
    ONESCB = CONSTB[:, 3616:3617]

    PL = {}
    off = 0
    for dirn in ("f", "b"):
        for nm, w in [("convw", 64), ("convb", 16), ("bdt", 16),
                      ("Dp", 16), ("lng", 8), ("lnb", 8), ("Alane", 512)]:
            PL[f"{nm}_{dirn}"] = (off, w)
            off += w
    for nm, w in [("bfuse", 8), ("n1", 8), ("n2", 8), ("MK2", 16),
                  ("NMROW", 16)]:
        PL[nm] = (off, w)
        off += w
    assert off <= 1344, off

    def prm(nm):
        o, w = PL[nm]
        return PRM[:, o:o + w]

    def ld(dst, src, waits=(), slow=False):
        def f(e, a=dst, b=src, s=slow):
            if s:
                return e.dma_start(a, b, allow_slow_non_contiguous=True)
            return e.dma_start(a, b)
        p.emit("sync", f, waits=waits, inc="d", inc_by=16)

    def st(dst, src, waits=()):
        p.emit("gpsimd", lambda e, a=dst, b=src: e.dma_start(a, b),
               waits=waits, inc="o", inc_by=16)

    H = dict(nc=nc, p=p, AB=AB, AFa=AFa, CONST=CONST, CONSTB=CONSTB,
             PRM=PRM, ROWS=ROWS, CAR=CAR, PSA=PSA, PSB=PSB, IDF=IDF,
             SELQ=SELQ, ONESR=ONESR, IDB=IDB, SELQB=SELQB, SELBC=SELBC,
             NSUM=NSUM, CMASK=CMASK, ONESCB=ONESCB, prm=prm, ld=ld, st=st,
             IN=IN, SC=SC, WBF=WBF, dirs=dirs, phases=phases,
             halves=halves, p1c=p1c, lite1=lite1, kadump=kadump)

    _p0(H)
    p.phase()
    if "mamba" in H["phases"]:
        for dirn in H["dirs"]:
            _mamba_dir(H, dirn)
    if "attn" in H["phases"]:
        _attn(H)
    if "moe" in H["phases"]:
        _moe(H)
    if "exit" in H["phases"]:
        _exit(H, OUT)
    for name, (dst, src) in DBG.items():
        r, c = src.shape
        st(_dap(dst, 0, [[c, r], [1, c]]), _dap(src, 0, [[c, r], [1, c]]),
           waits=p.after())
    p.emit("gpsimd", lambda e: e.nop(), waits=p.after("o", "d"))


def _ln_stats(H, xb_bf, sq_bf, rstd, mrs=None, eps=EPS_LN, rms=False):
    p, PSB, ROWS, ONESCB = H["p"], H["PSB"], H["ROWS"], H["ONESCB"]
    vc, sc = p.cnt["v"], p.cnt["s"]
    if not rms:
        for i in range(8):
            p.emit("tensor", lambda e, i=i: e.matmul(
                PSB[0:1, 0:512], ONESCB, xb_bf[:, i * 512:(i + 1) * 512],
                start=(i == 0), stop=(i == 7)),
                waits=[("v", vc), ("s", sc)] if i == 0 else (), inc="t")
    for i in range(8):
        p.emit("tensor", lambda e, i=i: e.matmul(
            PSB[0:1, 512:1024], ONESCB, sq_bf[:, i * 512:(i + 1) * 512],
            start=(i == 0), stop=(i == 7)),
            waits=[("v", vc), ("s", sc)] if i == 0 else (), inc="t")
    tc_ = p.cnt["t"]
    if rms:
        p.emit("scalar", lambda e: e.activation(
            ROWS[0:1, 3072:3584], PSB[0:1, 512:1024], AF.Ln, bias=eps,
            scale=1.0 / D), waits=[("t", tc_)], inc="s")
        p.emit("scalar", lambda e: e.activation(
            rstd, ROWS[0:1, 3072:3584], AF.Exp, scale=-0.5), inc="s")
        return
    mean = ROWS[0:1, 0:512]
    m2 = ROWS[0:1, 2048:2560]
    var = ROWS[0:1, 2560:3072]
    p.emit("vector", lambda e: e.tensor_scalar_mul(
        mean, PSB[0:1, 0:512], 1.0 / D), waits=[("t", tc_)], inc="v")
    p.emit("vector", lambda e: e.tensor_mul(m2, mean, mean), inc="v")
    p.emit("vector", lambda e: e.scalar_tensor_tensor(
        var, PSB[0:1, 512:1024], 1.0 / D, m2, ALU.mult, ALU.subtract),
        inc="v")
    vc = p.cnt["v"]
    p.emit("scalar", lambda e: e.activation(
        ROWS[0:1, 3072:3584], var, AF.Ln, bias=eps), waits=[("v", vc)], inc="s")
    p.emit("scalar", lambda e: e.activation(
        rstd, ROWS[0:1, 3072:3584], AF.Exp, scale=-0.5), inc="s")
    sc = p.cnt["s"]
    p.emit("vector", lambda e: e.tensor_mul(mrs, mean, rstd),
           waits=[("s", sc)], inc="v")


def _p0(H):
    p, AB, AFa = H["p"], H["AB"], H["AFa"]
    CONST, CONSTB, PRM, PSA = H["CONST"], H["CONSTB"], H["PRM"], H["PSA"]
    IN, SC, WBF = H["IN"], H["SC"], H["WBF"]
    ld, st, prm, IDF = H["ld"], H["st"], H["prm"], H["IDF"]

    STG = AFa[:, 0:4096]
    STG2 = AFa[:, 4096:8192]
    STGB = AB[:, 0:4096]

    ld(CONST[:, 0:128], IN["IDF32"])
    ld(CONST[:, 128:640], IN["SELQ32"])
    ld(CONST[0:1, 640:768], IN["ONESR"])
    ld(CONST[:, 769:777], IN["FCONST"])
    for dirn in ("f", "b"):
        for nm in ("convw", "convb", "bdt", "Dp", "lng", "lnb"):
            ld(prm(f"{nm}_{dirn}"), IN[f"{nm}_{dirn}"])
    for nm in ("bfuse", "n1", "n2"):
        ld(prm(nm), IN[nm])
    bfc = [("IDBF_f32", CONSTB[:, 0:128], 128, 128),
           ("SELQBF_f32", CONSTB[:, 128:640], 128, 512),
           ("SELBC_f32", CONSTB[64:128, 640:2688], 64, 2048),
           ("NSUM_f32", CONSTB[:, 2688:2720], 128, 32),
           ("CMASK_f32", CONSTB[:, 2720:3616], 128, 896),
           ("ONESCB_f32", CONSTB[:, 3616:3617], 128, 1)]
    for name, dst, pn, w in bfc:
        ld(STG2[0:pn, 0:w], IN[name], waits=p.after("v"))
        dc = p.cnt["d"]
        p.emit("vector",
               lambda e, d=dst, s=STG2[0:pn, 0:w]: e.tensor_copy(d, s),
               waits=[("d", dc)], inc="v")
    for dirn in ("f", "b"):
        ld(STG2[:, 0:512], IN[f"AlogL_{dirn}"], waits=p.after("v", "s"))
        dc = p.cnt["d"]
        p.emit("scalar",
               lambda e, s=STG2[:, 0:512], t=STG2[:, 512:1024]:
               e.activation(t, s, AF.Exp), waits=[("d", dc)], inc="s")
        sc = p.cnt["s"]
        p.emit("vector",
               lambda e, d=prm(f"Alane_{dirn}"), t=STG2[:, 512:1024]:
               e.tensor_scalar_mul(d, t, -1.0), waits=[("s", sc)], inc="v")
    p.emit("vector", lambda e: e.memset(prm("MK2"), 0.0), inc="v")

    wlist = [("WinT_f", D, 2 * E), ("WinT_b", D, 2 * E),
             ("WoutT_f", E, D), ("WoutT_b", E, D), ("WfuseT", 2 * D, D),
             ("qkvT", D, 3 * D), ("owT", D, D), ("upT", D, ED),
             ("downT", ED, D)]
    STGs = (STG, STG2)
    STGBs = (STGB, AB[:, 4096:8192])
    cp_mark = {}
    st_mark = {}
    idx = 0
    for nm, K, M in wlist:
        for kb in range(K // 128):
            for c0 in range(0, M, 4096):
                w = min(4096, M - c0)
                sl = idx % 2
                idx += 1
                src = _dap(IN[nm], kb * 128 * M + c0, [[M, 128], [1, w]])
                dstd = _dap(WBF[nm], kb * 128 * M + c0, [[M, 128], [1, w]])
                ld(STGs[sl][:, 0:w], src,
                   waits=cp_mark.get(sl, p.after("v")))
                dc = p.cnt["d"]
                p.emit("vector", lambda e, a=STGBs[sl][:, 0:w],
                       b=STGs[sl][:, 0:w]: e.tensor_copy(a, b),
                       waits=[("d", dc)]
                       + st_mark.get(sl, [("o", p.cnt["o"])]), inc="v")
                cp_mark[sl] = [("v", p.cnt["v"])]
                st(dstd, STGBs[sl][:, 0:w], waits=cp_mark[sl])
                st_mark[sl] = [("o", p.cnt["o"])]

    XT = STG2
    for c in range(NCH):
        src = _dap(IN["x"], c * TC * D, [[D, 128], [128 * D, 4], [1, D]])
        ld(STG[:, 0:4096], src, waits=p.after("t", "v", "o"))
        dc = p.cnt["d"]
        for tb in range(4):
            for db in range(8):
                k = (tb * 8 + db) % 4
                pt = PSA[:, k * 512: k * 512 + 128]
                p.emit("tensor",
                       lambda e, o=pt, tb=tb, db=db:
                       e.transpose(o, STG[:, tb * 1024 + db * 128:
                                          tb * 1024 + db * 128 + 128],
                                   IDF),
                       waits=[("d", dc), ("s", p.cnt["s"])], inc="t")
                tc_ = p.cnt["t"]
                p.emit("scalar",
                       lambda e, o=XT[:, db * 512 + tb * 128:
                                      db * 512 + tb * 128 + 128], i=pt:
                       e.copy(o, i), waits=[("t", tc_)], inc="s")
        sc = p.cnt["s"]
        st(_dap(SC["xF"], c * TC, [[T, 128], [128 * T, 8], [1, TC]]),
           XT[:, 0:4096], waits=[("s", sc)])


def _mamba_dir(H, dirn):
    p, AB, AFa = H["p"], H["AB"], H["AFa"]
    ROWS, CAR, PSA, PSB = H["ROWS"], H["CAR"], H["PSA"], H["PSB"]
    IN, SC, WBF = H["IN"], H["SC"], H["WBF"]
    ld, st, prm = H["ld"], H["st"], H["prm"]
    SELQ, SELQB, SELBC, NSUM = H["SELQ"], H["SELQB"], H["SELBC"], H["NSUM"]
    ONESR, ONESCB = H["ONESR"], H["ONESCB"]

    WIN = AB[:, 0:32768]
    LN0 = AB[:, 32768:36864]
    LR = AB[:, 36864:40960]
    UC = AB[:, 40960:49152]
    ZS = AB[:, 49152:57344]
    SQ = ZS[:, 0:4096]
    UB = AB[:, 57344:65584]
    WX = AB[:, 65584:67632]
    DTR = AB[0:64, AB_TAIL:AB_TAIL + 4096]
    BCR = AB[64:128, AB_TAIL:AB_TAIL + 4096]
    XS = AFa[:, 0:4096]
    CB = AFa[:, 4096:4608]

    win_nm = f"WinT_{dirn}"
    cw = prm(f"convw_{dirn}")
    cb_ = prm(f"convb_{dirn}")
    bdt = prm(f"bdt_{dirn}")
    Dp = prm(f"Dp_{dirn}")
    Alane = prm(f"Alane_{dirn}")

    ld(WIN, _dap(WBF[win_nm], 0, [[2 * E, 128], [128 * 2 * E, 8],
                                  [1, 4096]]), waits=p.after())
    ld(AFa[:, 4608:6656], _dap(IN[f"WxT_{dirn}"], 0,
                               [[128, 128], [128 * 128, 16], [1, 128]]))
    dc = p.cnt["d"]
    p.emit("vector", lambda e, a=WX, b=AFa[:, 4608:6656]:
           e.tensor_copy(a, b), waits=[("d", dc)], inc="v")
    p.emit("vector", lambda e: e.memset(CAR[:, 512:561], 0.0), inc="v")

    LNT = AFa[:, 4608:5120]
    xs_free = p.after("v", "s", "t", "o")
    ln0_free = []
    lr_free = []
    stats_t = []
    silu_prev = p.after("o", "s")
    bank_mark = {}
    uspill = []
    zspill = []
    xdbl15 = []
    silu_mark = {}
    for c in range(NCH):
        if dirn == "f":
            ld(XS, _dap(SC["xF"], c * TC, [[T, 128], [128 * T, 8],
                                           [1, TC]]), waits=xs_free)
            dc = p.cnt["d"]
            p.emit("vector", lambda e: e.tensor_copy(LR, XS),
                   waits=[("d", dc)] + stats_t, inc="v")
            p.emit("scalar", lambda e: e.activation(SQ, XS, AF.Square),
                   waits=[("d", dc)] + zspill, inc="s")
            rstd = ROWS[0:1, 2048:2560]
            mrs = ROWS[0:1, 2560:3072]
            _ln_stats(H, LR, SQ, rstd, mrs)
            stats_t = [("t", p.cnt["t"])]
            vc, sc = p.cnt["v"], p.cnt["s"]
            p.emit("tensor", lambda e: e.matmul(
                PSA[:, 0:512], ONESR, rstd, start=True, stop=True),
                waits=[("v", vc), ("s", sc)], inc="t")
            p.emit("tensor", lambda e: e.matmul(
                PSA[:, 512:1024], ONESR, mrs, start=True, stop=True),
                inc="t")
            tc_ = p.cnt["t"]
            for i in range(8):
                p.emit("vector", lambda e, i=i: e.tensor_mul(
                    LNT, XS[:, i * 512:(i + 1) * 512], PSA[:, 0:512]),
                    waits=([("t", tc_)] + ln0_free) if i == 0
                    else [("t", tc_)], inc="v")
                p.emit("vector", lambda e, i=i: e.tensor_sub(
                    LN0[:, i * 512:(i + 1) * 512], LNT,
                    PSA[:, 512:1024]), inc="v")
            ln0_ready = p.cnt["v"]
            xs_free = [("v", ln0_ready), ("s", p.cnt["s"])]
            st(_dap(SC["ln0F"], c * TC, [[T, 128], [128 * T, 8], [1, TC]]),
               LN0, waits=[("v", ln0_ready)])
            ln0_st = [("o", p.cnt["o"])]
        else:
            cg = NCH - 1 - c
            ld(LR, _dap(SC["ln0F"], cg * TC, [[T, 128], [128 * T, 8],
                                              [1, TC]]),
               waits=lr_free if lr_free else p.after("v", "s", "t", "o"))
            dc = p.cnt["d"]
            for i in range(8):
                rev = bass.AP(LR.tensor, LR.offset + i * 512 + 511,
                              [[AB_N, 128], [-1, 512]])
                p.emit("vector", lambda e, i=i, r=rev: e.tensor_copy(
                    LN0[:, i * 512:(i + 1) * 512], r),
                    waits=([("d", dc)] + ln0_free) if i == 0
                    else [("d", dc)], inc="v")
            ln0_ready = p.cnt["v"]
            lr_free = [("v", ln0_ready)]
            ln0_st = []

        def xz_group(eb, first_w):
            bank = PSA[:, (eb % 4) * 512:((eb % 4) + 1) * 512]
            for kb in range(8):
                p.emit("tensor", lambda e, eb=eb, kb=kb, bk=bank:
                       e.matmul(
                           bk, WIN[:, kb * 4096 + eb * 128:
                                   kb * 4096 + eb * 128 + 128],
                           LN0[:, kb * 512:(kb + 1) * 512],
                           start=(kb == 0), stop=(kb == 7)),
                       waits=first_w if kb == 0 else (),
                       inc="t" if kb == 7 else None)
            return bank

        def x_dbl(kb):
            p.emit("tensor", lambda e, kb=kb: e.matmul(
                PSB[:, 1024:1536], WX[:, kb * 128:(kb + 1) * 128],
                UC[:, kb * 512:(kb + 1) * 512],
                start=(kb == 0), stop=(kb == 15)),
                waits=silu_mark[kb], inc="t" if kb == 15 else None)

        for sl in range(16):
            fw = bank_mark.pop(sl % 4, None)
            if fw is None:
                fw = p.after("s", "v")
            if sl == 0:
                fw = fw + [("v", ln0_ready)]
            bank = xz_group(sl, fw)
            tg = p.cnt["t"]
            p.emit("vector", lambda e, sl=sl, bk=bank: e.tensor_copy(
                UB[:, sl * 515 + 3: sl * 515 + 515], bk),
                waits=[("t", tg)], inc="v")
            bank_mark[sl % 4] = [("v", p.cnt["v"])]
            p.emit("vector", lambda e, j=sl: e.tensor_copy(
                UB[:, j * 515: j * 515 + 3],
                CAR[:, 512 + j * 3: 512 + j * 3 + 3]), inc="v")
            p.emit("vector", lambda e, j=sl: e.tensor_scalar(
                CB, UB[:, j * 515: j * 515 + 512],
                cw[:, j * 4: j * 4 + 1], cb_[:, j: j + 1],
                ALU.mult, ALU.add),
                waits=silu_prev if sl == 0
                else silu_mark[sl - 1], inc="v")
            for k in range(1, 4):
                p.emit("vector", lambda e, j=sl, k=k:
                       e.scalar_tensor_tensor(
                           CB, UB[:, j * 515 + k: j * 515 + k + 512],
                           cw[:, j * 4 + k: j * 4 + k + 1], CB,
                           ALU.mult, ALU.add), inc="v")
            p.emit("vector", lambda e, j=sl: e.tensor_copy(
                CAR[:, 512 + j * 3: 512 + j * 3 + 3],
                UB[:, j * 515 + 512: j * 515 + 515]), inc="v")
            vc = p.cnt["v"]
            sw = [("v", vc)]
            if sl == 0:
                sw = sw + xdbl15 + uspill
            p.emit("scalar", lambda e, j=sl: e.activation(
                UC[:, j * 512:(j + 1) * 512], CB, AF.Silu),
                waits=sw, inc="s")
            silu_mark[sl] = [("s", p.cnt["s"])]
            if sl >= 2:
                x_dbl(sl - 2)
        x_dbl(14)
        x_dbl(15)
        tc_ = p.cnt["t"]
        xdbl15 = [("t", tc_)]
        p.emit("scalar", lambda e, c=c: e.copy(
            DTR[:, c * 512:(c + 1) * 512], PSB[0:64, 1024:1536]),
            waits=[("t", tc_)], inc="s")
        p.emit("scalar", lambda e, c=c: e.copy(
            BCR[:, c * 512:(c + 1) * 512], PSB[64:128, 1024:1536]),
            inc="s")
        sc = p.cnt["s"]
        st(_dap(SC["ucF"], c * TC, [[T, 128], [128 * T, 16], [1, TC]]),
           UC, waits=[("s", sc)])
        uspill = [("o", p.cnt["o"])]
        for eb in range(16, 32):
            fw = bank_mark.pop(eb % 4)
            bank = xz_group(eb, fw)
            tg = p.cnt["t"]
            zw = [("t", tg)]
            if eb == 16:
                zw = zw + zspill
            p.emit("scalar", lambda e, eb=eb, bk=bank: e.activation(
                ZS[:, (eb - 16) * 512:(eb - 15) * 512], bk, AF.Silu),
                waits=zw, inc="s")
            bank_mark[eb % 4] = [("s", p.cnt["s"])]
        ln0_free = [("t", p.cnt["t"])] + ln0_st
        sc = p.cnt["s"]
        st(_dap(SC["zsF"], c * TC, [[T, 128], [128 * T, 16], [1, TC]]),
           ZS[:, 0:8192], waits=[("s", sc)])
        zspill = [("o", p.cnt["o"])]
        silu_prev = silu_mark[15]
    p.phase()

    BREP = [AB[:, ng * 2048:(ng + 1) * 2048] for ng in range(8)]
    CREP = [AB[:, 16384 + ng * 2048: 16384 + (ng + 1) * 2048]
            for ng in range(8)]
    DTU4 = AB[:, 32768:40960]
    BBUF = [AB[:, 40960 + i * 2048: 43008 + i * 2048] for i in range(2)]
    HBUF = [AB[:, 45056 + i * 2048: 47104 + i * 2048] for i in range(2)]
    YBUF = [AB[:, 49152 + i * 2048: 51200 + i * 2048] for i in range(2)]
    UCJ2 = [AB[:, 53248:55296], AB[:, 63488:65536]]
    ZSJ2 = [AB[:, 55296:57344], AB[:, 65536:67584]]
    YG = AB[:, 57344:59392]
    DTUB = AB[:, 59392:61440]
    WDT = AB[0:64, 61440:63488]
    ZF = AFa[:, 0:2048]
    NZ = AFa[:, 2048:4096]
    DTJ = AFa[:, 4096:6144]
    ABUF = [AFa[:, 6144:8192], AFa[:, 2048:4096]]
    Y2 = ZF

    ld(AFa[0:64, 0:2048], _dap(IN[f"WdtT_{dirn}"], 0,
                                   [[E, 64], [1, 2048]]),
       waits=p.after())
    dc = p.cnt["d"]
    p.emit("vector", lambda e, a=WDT, b=AFa[0:64, 0:2048]:
           e.tensor_copy(a, b), waits=[("d", dc)], inc="v")
    p.phase()

    ucj_guard = {}
    for hf in H["halves"]:
        bc_lo = None
        bc_hi = None
        for ng in range(8):
            for which, dst in ((0, BREP[ng]), (1, CREP[ng])):
                for s4 in range(2):
                    p.emit("tensor", lambda e, ng=ng, w=which, s4=s4,
                           hf=hf: e.matmul(
                               PSA[:, s4 * 512:(s4 + 1) * 512],
                               SELBC[:, (ng * 2 + w) * 128:
                                     (ng * 2 + w) * 128 + 128],
                               BCR[:, hf * 2048 + s4 * 512:
                                   hf * 2048 + s4 * 512 + 512],
                               start=True, stop=True),
                           waits=(p.after("s", "v", "g")
                                  if bc_lo is None and s4 == 0 else
                                  (bc_lo if s4 == 0 else ())), inc="t")
                tlo = p.cnt["t"]
                for s4 in range(2, 4):
                    p.emit("tensor", lambda e, ng=ng, w=which, s4=s4,
                           hf=hf: e.matmul(
                               PSA[:, s4 * 512:(s4 + 1) * 512],
                               SELBC[:, (ng * 2 + w) * 128:
                                     (ng * 2 + w) * 128 + 128],
                               BCR[:, hf * 2048 + s4 * 512:
                                   hf * 2048 + s4 * 512 + 512],
                               start=True, stop=True),
                           waits=(bc_hi if s4 == 2 and bc_hi else ()),
                           inc="t")
                thi = p.cnt["t"]
                p.emit("scalar", lambda e, d=dst[:, 0:1024]:
                       e.copy(d, PSA[:, 0:1024]),
                       waits=[("t", tlo)], inc="s")
                bc_lo = [("s", p.cnt["s"])]
                p.emit("scalar", lambda e, d=dst[:, 1024:2048]:
                       e.copy(d, PSA[:, 1024:2048]),
                       waits=[("t", thi)], inc="s")
                bc_hi = [("s", p.cnt["s"])]

        for j in range(NJ):
            UCJ = UCJ2[j % 2]
            ZSJ = ZSJ2[j % 2]
            ld(UCJ, _dap(SC["ucF"], j * 128 * T + hf * HF,
                         [[T, 128], [1, HF]]),
               waits=ucj_guard.get(j % 2, []))
            ld(ZSJ, _dap(SC["zsF"], j * 128 * T + hf * HF,
                         [[T, 128], [1, HF]]),
               waits=ucj_guard.get(j % 2, []))
            dcj = p.cnt["d"]
            for s4 in range(4):
                p.emit("tensor", lambda e, s4=s4, j=j, hf=hf: e.matmul(
                    PSA[:, s4 * 512:(s4 + 1) * 512],
                    WDT[:, j * 128:(j + 1) * 128],
                    DTR[:, hf * 2048 + s4 * 512:
                        hf * 2048 + s4 * 512 + 512],
                    start=True, stop=True),
                    waits=p.after("s", "v") if s4 == 0 else (), inc="t")
                tc_ = p.cnt["t"]
                p.emit("scalar", lambda e, s4=s4, j=j: e.activation(
                    ZF[:, s4 * 512:(s4 + 1) * 512],
                    PSA[:, s4 * 512:(s4 + 1) * 512], AF.Identity,
                    bias=bdt[:, j: j + 1]), waits=[("t", tc_)], inc="s")
            sc = p.cnt["s"]
            p.emit("vector", lambda e: e.tensor_single_scalar(
                NZ, ZF, -1.0, ALU.mult),
                waits=[("s", sc)] + p.after("t"), inc="v")
            p.emit("vector", lambda e: e.tensor_tensor(
                NZ, ZF, NZ, ALU.min), inc="v")
            vc = p.cnt["v"]
            p.emit("scalar", lambda e: e.activation(NZ, NZ, AF.Exp),
                   waits=[("v", vc)], inc="s")
            p.emit("scalar", lambda e: e.activation(NZ, NZ, AF.Ln,
                                                    bias=1.0), inc="s")
            sc = p.cnt["s"]
            p.emit("vector", lambda e: e.scalar_tensor_tensor(
                DTJ, ZF, 0.0, NZ, ALU.max, ALU.add),
                waits=[("s", sc)], inc="v")
            p.emit("vector", lambda e, UCJ=UCJ: e.tensor_mul(
                DTUB, DTJ, UCJ), waits=[("d", dcj)], inc="v")
            vdt = p.cnt["v"]
            if hf == 1 and H.get("lite1"):
                p.phase()
                continue
            tdq_marks = {}
            aexp7_pred = {}

            def emit_dtrep(dq, w):
                for s4 in range(4):
                    p.emit("tensor", lambda e, dq=dq, s4=s4: e.matmul(
                        PSA[:, s4 * 512:(s4 + 1) * 512],
                        SELQ[:, dq * 128:(dq + 1) * 128],
                        DTJ[:, s4 * 512:(s4 + 1) * 512],
                        start=True, stop=True),
                        waits=w if s4 == 0 else (), inc="t")
                tdq_marks[dq] = p.cnt["t"]

            emit_dtrep(0, [("v", vdt)] + p.after("s"))
            prev_lo = None
            prev_hi = None
            for dq in range(4):
                for s4 in range(2):
                    p.emit("tensor", lambda e, dq=dq, s4=s4: e.matmul(
                        PSB[:, s4 * 512:(s4 + 1) * 512],
                        SELQB[:, dq * 128:(dq + 1) * 128],
                        DTUB[:, s4 * 512:(s4 + 1) * 512],
                        start=True, stop=True),
                        waits=(([("v", vdt)] + p.after("s", "g"))
                               if dq == 0 and s4 == 0 else
                               (prev_lo if s4 == 0 else ())), inc="t")
                tlo = p.cnt["t"]
                for s4 in range(2, 4):
                    p.emit("tensor", lambda e, dq=dq, s4=s4: e.matmul(
                        PSB[:, s4 * 512:(s4 + 1) * 512],
                        SELQB[:, dq * 128:(dq + 1) * 128],
                        DTUB[:, s4 * 512:(s4 + 1) * 512],
                        start=True, stop=True),
                        waits=(prev_hi if s4 == 2 and dq > 0 else ()),
                        inc="t")
                thi = p.cnt["t"]
                p.emit("scalar", lambda e, dq=dq: e.copy(
                    DTU4[:, dq * 2048: dq * 2048 + 1024],
                    PSB[:, 0:1024]), waits=[("t", tlo)], inc="s")
                prev_lo = [("s", p.cnt["s"])]
                p.emit("scalar", lambda e, dq=dq: e.copy(
                    DTU4[:, dq * 2048 + 1024:(dq + 1) * 2048],
                    PSB[:, 1024:2048]), waits=[("t", thi)], inc="s")
                prev_hi = [("s", p.cnt["s"])]
            it = 0
            scan_marks = {}
            yred_marks = {}
            for dq in range(4):
                tdq = tdq_marks[dq]
                for ng in range(8):
                    u = (j * 4 + dq) * 8 + ng
                    sl = it % 2
                    it += 1
                    p.emit("scalar", lambda e, u=u, sl=sl: e.activation(
                        ABUF[sl], PSA[:, 0:2048], AF.Exp,
                        scale=Alane[:, u: u + 1]),
                        waits=[("t", tdq)] + scan_marks.get(it - 2, []),
                        inc="s")
                    sa = p.cnt["s"]
                    if ng == 7:
                        assert aexp7_pred.pop(dq, sa) == sa
                    p.emit("vector", lambda e, dq=dq, ng=ng, sl=sl:
                           e.tensor_mul(BBUF[sl],
                                        DTU4[:, dq * 2048:
                                             (dq + 1) * 2048],
                                        BREP[ng]),
                           waits=[("s", sa)], inc="v")
                    ini = 0.0 if hf == 0 else CAR[:, u: u + 1]
                    p.emit("vector", lambda e, sl=sl, ini=ini:
                           e.tensor_tensor_scan(HBUF[sl], ABUF[sl],
                                                BBUF[sl], ini,
                                                ALU.mult, ALU.add),
                           waits=p.after("s"), inc="v")
                    scan_marks[it] = [("v", p.cnt["v"])]
                    vh = p.cnt["v"]
                    if hf == 0:
                        p.emit("scalar", lambda e, u=u, sl=sl: e.copy(
                            CAR[:, u: u + 1], HBUF[sl][:, HF - 1: HF]),
                            waits=[("v", vh)], inc="s")
                    p.emit("vector", lambda e, ng=ng, sl=sl: e.tensor_mul(
                        YBUF[sl], HBUF[sl], CREP[ng]),
                        waits=yred_marks.get(it - 2, []), inc="v")
                    vy = p.cnt["v"]
                    for s4 in range(4):
                        p.emit("tensor", lambda e, dq=dq, s4=s4, ng=ng,
                               sl=sl: e.matmul(
                                   PSB[32 * dq: 32 * dq + 32,
                                       s4 * 512:(s4 + 1) * 512], NSUM,
                                   YBUF[sl][:, s4 * 512:(s4 + 1) * 512],
                                   start=(ng == 0), stop=(ng == 7),
                                   tile_position=(0, 32 * dq)),
                               waits=[("v", vy)] if s4 == 0 else (),
                               inc="t" if s4 == 3 else None)
                    yred_marks[it] = [("t", p.cnt["t"])]
                    if ng == 4 and dq < 3:
                        pred = p.cnt["s"] + (5 if hf == 0 else 3)
                        aexp7_pred[dq] = pred
                        emit_dtrep(dq + 1, [("s", pred)])
            ty = p.cnt["t"]
            p.emit("vector", lambda e, j=j, UCJ=UCJ: e.scalar_tensor_tensor(
                Y2, UCJ, Dp[:, j: j + 1], PSB[:, 0:2048], ALU.mult,
                ALU.add), waits=[("t", ty)], inc="v")
            p.emit("vector", lambda e, ZSJ=ZSJ: e.tensor_mul(YG, Y2, ZSJ),
                   waits=p.after("o"), inc="v")
            vc = p.cnt["v"]
            ucj_guard[j % 2] = [("v", vc)]
            st(_dap(SC["ygF"], j * 128 * T + hf * HF, [[T, 128], [1, HF]]),
               YG, waits=[("v", vc)])

    WOUT = AB[:, 0:16384]
    YGC = AB[:, 16384:24576]
    FOLN = AB[:, 24576:28672]
    FOB = AB[:, 28672:32768]
    SQ2 = AB[:, 32768:36864]
    WFUSE = AB[:, 36864:53248]
    FOLD = AB[:, 53248:57344]
    BOREV = AB[:, 57344:61440]
    FO = AFa[:, 0:4096]
    XC = AFa[:, 4096:8192]
    TMP = AFa[:, 8192:8704]

    if not H["p1c"]:
        return
    ld(WOUT, _dap(WBF["WoutT_" + dirn], 0, [[D, 128], [128 * D, 16],
                                            [1, 1024]]), waits=p.after())
    if dirn == "b":
        ld(WFUSE, _dap(WBF["WfuseT"], 0, [[D, 128], [128 * D, 16],
                                          [1, 1024]]))
    gcol = prm(f"lng_{dirn}")
    bcol = prm(f"lnb_{dirn}")
    bfu = prm("bfuse")
    TMP2 = AB.bitcast(F32)[:, 30720:31232]
    p.phase()
    ygc_free = p.after("t")
    fo_free = []
    fob_free = []
    fost = []
    xcld = []
    bank_free = {}
    for c in range(NCH):
        ld(YGC, _dap(SC["ygF"], c * TC, [[T, 128], [128 * T, 16],
                                         [1, TC]]), waits=ygc_free)
        dc = p.cnt["d"]
        for db in range(8):
            bank = PSA[:, (db % 4) * 512:((db % 4) + 1) * 512]
            for kb in range(NJ):
                p.emit("tensor", lambda e, db=db, kb=kb, bk=bank: e.matmul(
                    bk, WOUT[:, kb * 1024 + db * 128: kb * 1024 + db * 128
                             + 128], YGC[:, kb * 512:(kb + 1) * 512],
                    start=(kb == 0), stop=(kb == 15)),
                    waits=([("d", dc)]
                           + bank_free.get(db % 4, p.after("s", "v")))
                    if kb == 0 else (), inc="t" if kb == 15 else None)
            tc_ = p.cnt["t"]
            p.emit("scalar", lambda e, db=db, bk=bank: e.copy(
                FO[:, db * 512:(db + 1) * 512], bk),
                waits=[("t", tc_)] + (fo_free if db == 0 else []),
                inc="s")
            bank_free[db % 4] = [("s", p.cnt["s"])]
        ygc_free = [("t", p.cnt["t"])]
        sc = p.cnt["s"]
        p.emit("vector", lambda e: e.tensor_copy(FOB, FO),
               waits=[("s", sc)] + fob_free, inc="v")
        p.emit("scalar", lambda e: e.activation(SQ2, FO, AF.Square),
               inc="s")
        rstd = ROWS[0:1, 2048:2560]
        mrs = ROWS[0:1, 2560:3072]
        _ln_stats(H, FOB, SQ2, rstd, mrs)
        fob_free = [("t", p.cnt["t"])]
        vc, sc = p.cnt["v"], p.cnt["s"]
        p.emit("tensor", lambda e: e.matmul(
            PSB[:, 1024:1536], ONESR, rstd, start=True, stop=True),
            waits=[("v", vc), ("s", sc)], inc="t")
        p.emit("tensor", lambda e: e.matmul(
            PSB[:, 1536:2048], ONESR, mrs, start=True, stop=True), inc="t")
        tc_ = p.cnt["t"]
        for db in range(8):
            p.emit("vector", lambda e, db=db: e.tensor_mul(
                TMP, FO[:, db * 512:(db + 1) * 512], PSB[:, 1024:1536]),
                waits=[("t", tc_)] + (fost if db == 0 else []), inc="v")
            p.emit("vector", lambda e: e.tensor_sub(
                TMP, TMP, PSB[:, 1536:2048]), inc="v")
            p.emit("vector", lambda e, db=db: e.tensor_scalar(
                FOLN[:, db * 512:(db + 1) * 512], TMP,
                gcol[:, db: db + 1], bcol[:, db: db + 1],
                ALU.mult, ALU.add), inc="v")
        vc = p.cnt["v"]
        fo_free = [("v", vc)]
        if dirn == "f":
            st(_dap(SC["foF"], c * TC, [[T, 128], [128 * T, 8], [1, TC]]),
               FOLN, waits=[("v", vc)])
            fost = [("o", p.cnt["o"])]
        else:
            cg = NCH - 1 - c
            for i in range(8):
                rev = bass.AP(FOLN.tensor, FOLN.offset + i * 512 + 511,
                              [[AB_N, 128], [-1, 512]])
                p.emit("vector", lambda e, i=i, r=rev: e.tensor_copy(
                    BOREV[:, i * 512:(i + 1) * 512], r),
                    waits=xcld if i == 0 else (), inc="v")
            ld(FOLD, _dap(SC["foF"], cg * TC, [[T, 128], [128 * T, 8],
                                               [1, TC]]), waits=xcld)
            ld(XC, _dap(SC["xF"], cg * TC, [[T, 128], [128 * T, 8],
                                            [1, TC]]))
            dc = p.cnt["d"]
            vrev = p.cnt["v"]
            tmps = (TMP, TMP2)
            add_mark = {}
            for db in range(8):
                bank = PSA[:, (db % 4) * 512:((db % 4) + 1) * 512]
                for kb in range(16):
                    rhs = (FOLD[:, kb * 512:(kb + 1) * 512] if kb < 8
                           else BOREV[:, (kb - 8) * 512:(kb - 7) * 512])
                    p.emit("tensor", lambda e, kb=kb, db=db, bk=bank,
                           rhs=rhs: e.matmul(
                               bk, WFUSE[:, kb * 1024 + db * 128:
                                         kb * 1024 + db * 128 + 128],
                               rhs, start=(kb == 0), stop=(kb == 15)),
                           waits=(([("d", dc), ("v", vrev)]
                                   + bank_free.get(db % 4, []))
                                  if kb == 0 else ()),
                           inc="t" if kb == 15 else None)
                tc_ = p.cnt["t"]
                tb = tmps[db % 2]
                p.emit("scalar", lambda e, db=db, bk=bank, tb=tb:
                       e.activation(tb, bk, AF.Identity,
                                    bias=bfu[:, db: db + 1]),
                       waits=[("t", tc_)] + add_mark.get(db - 2, []),
                       inc="s")
                bank_free[db % 4] = [("s", p.cnt["s"])]
                sc2 = p.cnt["s"]
                p.emit("vector", lambda e, db=db, tb=tb: e.tensor_add(
                    XC[:, db * 512:(db + 1) * 512],
                    XC[:, db * 512:(db + 1) * 512], tb),
                    waits=[("s", sc2)], inc="v")
                add_mark[db] = [("v", p.cnt["v"])]
            vc = p.cnt["v"]
            st(_dap(SC["xF"], cg * TC, [[T, 128], [128 * T, 8], [1, TC]]),
               XC, waits=[("v", vc)])
            xcld = [("v", vc), ("o", p.cnt["o"])]
    p.phase()


def _attn(H):
    p, AB, AFa = H["p"], H["AB"], H["AFa"]
    ROWS, PSA, PSB = H["ROWS"], H["PSA"], H["PSB"]
    IN, SC, WBF = H["IN"], H["SC"], H["WBF"]
    ld, st, prm = H["ld"], H["st"], H["prm"]
    IDF, IDB, ONESR, ONESCB, CMASK = (H["IDF"], H["IDB"], H["ONESR"],
                                      H["ONESCB"], H["CMASK"])

    KA = AB[:, 0:32768]
    QKVW = [AB[:, 32768:35840], AB[:, 35840:38912]]
    QS = AB[:, 38912:43008]
    LN1 = AB[:, 43008:47104]
    OWT = AB[:, 47104:55296]
    ATO = AB[:, 55296:59392]
    VSTG = AB[:, 59392:63488]
    VTS = AB[:, 63488:67712]
    EBUF = [AB[:, 67632:68144], AB[:, 68144:68656]]
    EM = AB[:, 68656:69168]
    SQB = AB[:, 68656:69168]
    VTB = [AB[:, 69168:71280]]
    XI = AFa[:, 0:4096]
    X2 = AFa[:, 4096:8192]
    AVS = AFa[0:64, 8192:8704]
    NMC = AFa[:, 8704:8720]
    TMQ = AFa[:, 4096:4608]
    RL = ROWS[0:1, 3072:3584]
    K2R = ROWS[0:1, 3072:3584]
    MK2 = prm("MK2")
    NMR = prm("NMROW")
    n1 = prm("n1")

    ld(OWT, _dap(WBF["owT"], 0, [[D, 128], [128 * D, 8], [1, 1024]]),
       waits=p.after())
    p.phase()

    av_t = {}
    d_vtb_last = [0]
    qkv_mark = {}
    VTB2 = [VTB[0], VSTG[:, 0:2112]]
    for c in range(NCH):
        ld(XI, _dap(SC["xF"], c * TC, [[T, 128], [128 * T, 8], [1, TC]]),
           waits=p.after("v", "s", "t"))
        dc = p.cnt["d"]
        p.emit("vector", lambda e: e.tensor_copy(VSTG[:, 0:4096], XI),
               waits=[("d", dc), ("o", p.cnt["o"])]
               + av_t.get(c * NH - 1, []), inc="v")
        p.emit("scalar", lambda e: e.activation(ATO[:, 0:4096], XI,
                                                AF.Square),
               waits=[("d", dc)], inc="s")
        rstd = ROWS[0:1, 2048:2560]
        mrs = ROWS[0:1, 2560:3072]
        _ln_stats(H, VSTG[:, 0:4096], ATO[:, 0:4096], rstd, mrs)
        vc, sc = p.cnt["v"], p.cnt["s"]
        p.emit("tensor", lambda e: e.matmul(
            PSA[:, 0:512], ONESR, rstd, start=True, stop=True),
            waits=[("v", vc), ("s", sc)], inc="t")
        p.emit("tensor", lambda e: e.matmul(
            PSA[:, 512:1024], ONESR, mrs, start=True, stop=True), inc="t")
        tc_ = p.cnt["t"]
        for i in range(8):
            p.emit("vector", lambda e, i=i: e.tensor_mul(
                TMQ, XI[:, i * 512:(i + 1) * 512], PSA[:, 0:512]),
                waits=[("t", tc_)], inc="v")
            p.emit("vector", lambda e, i=i: e.tensor_sub(
                XI[:, i * 512:(i + 1) * 512], TMQ, PSA[:, 512:1024]),
                inc="v")
        p.emit("scalar", lambda e: e.activation(ATO[:, 0:4096], XI,
                                                AF.Square),
               waits=[("v", p.cnt["v"])], inc="s")
        _ln_stats(H, None, ATO[:, 0:4096], rstd, rms=True, eps=EPS_RMS)
        sc = p.cnt["s"]
        p.emit("tensor", lambda e: e.matmul(
            PSA[:, 0:512], ONESR, rstd, start=True, stop=True),
            waits=[("s", sc)], inc="t")
        tc_ = p.cnt["t"]
        for i in range(8):
            p.emit("vector", lambda e, i=i: e.tensor_mul(
                TMQ, XI[:, i * 512:(i + 1) * 512], PSA[:, 0:512]),
                waits=[("t", tc_)], inc="v")
            p.emit("vector", lambda e, i=i: e.tensor_scalar_mul(
                LN1[:, i * 512:(i + 1) * 512], TMQ, n1[:, i: i + 1]),
                inc="v")
        vln1 = p.cnt["v"]
        for grp in range(3):
            for kb in range(8):
                src = _dap(WBF["qkvT"], kb * 128 * 3 * D + grp * 1024,
                           [[3 * D, 128], [1, 1024]])
                qw = qkv_mark.get(kb % 2) or p.after("t")
                if kb % 2 == 0:
                    ld(QKVW[0][:, 0:1024], src, waits=qw)
                    dm = ("d", p.cnt["d"])
                else:
                    st(QKVW[1][:, 0:1024], src, waits=qw)
                    dm = ("o", p.cnt["o"])
                for e8 in range(8):
                    bank = (PSA if e8 < 4 else PSB)[
                        :, (e8 % 4) * 512:((e8 % 4) + 1) * 512]
                    p.emit("tensor", lambda e, kb=kb, e8=e8, grp=grp,
                           bk=bank: e.matmul(
                               bk, QKVW[kb % 2][:, e8 * 128:
                                                e8 * 128 + 128],
                               LN1[:, kb * 512:(kb + 1) * 512],
                               start=(kb == 0), stop=(kb == 7)),
                           waits=(([("v", vln1), dm]
                                   + p.after("s")) if kb == 0 else
                                  [dm] if e8 == 0 else ()),
                           inc="t" if e8 == 7 else None)
                qkv_mark[kb % 2] = [("t", p.cnt["t"])]
            tc_ = p.cnt["t"]
            for e8 in range(8):
                bank = (PSA if e8 < 4 else PSB)[
                    :, (e8 % 4) * 512:((e8 % 4) + 1) * 512]
                if grp == 0:
                    dst = QS[:, e8 * 512:(e8 + 1) * 512]
                elif grp == 1:
                    dst = KA[:, e8 * 4096 + c * 512: e8 * 4096 + c * 512
                             + 512]
                else:
                    dst = VSTG[:, e8 * 512:(e8 + 1) * 512]
                if grp <= 1:
                    p.emit("scalar", lambda e, d=dst, bk=bank:
                           e.activation(d, bk, AF.Identity,
                                        scale=0.3535533905932738),
                           waits=[("t", tc_)], inc="s")
                else:
                    gw = [("t", tc_)]
                    if e8 == 0:
                        gw = gw + [("d", d_vtb_last[0])]
                    p.emit("scalar", lambda e, d=dst, bk=bank:
                           e.copy(d, bk), waits=gw, inc="s")
        sqkv = p.cnt["s"]
        p.emit("vector", lambda e: e.memset(VTS, 1.0),
               waits=p.after("o"), inc="v")
        vm = p.cnt["v"]
        for r in range(4):
            for vb in range(8):
                bank = PSA[:, ((r * 8 + vb) % 2) * 512:
                           ((r * 8 + vb) % 2) * 512 + 512]
                ptb = bank.bitcast(BF16)[:, 0:128]
                p.emit("tensor", lambda e, r=r, vb=vb, o=ptb: e.transpose(
                    o, VSTG[:, vb * 512 + r * 128: vb * 512 + r * 128
                            + 128], IDB),
                    waits=[("s", sqkv), ("v", vm)] if (r == 0 and vb == 0)
                    else p.after("s"), inc="t")
                tc_ = p.cnt["t"]
                p.emit("scalar", lambda e, r=r, vb=vb, i=ptb: e.copy(
                    VTS[:, r * 1056 + (2 * vb) * 66:
                        r * 1056 + (2 * vb) * 66 + 64], i[:, 0:64]),
                    waits=[("t", tc_)], inc="s")
                p.emit("scalar", lambda e, r=r, vb=vb, i=ptb: e.copy(
                    VTS[:, r * 1056 + (2 * vb + 1) * 66:
                        r * 1056 + (2 * vb + 1) * 66 + 64],
                    i[:, 64:128]), inc="s")
        scv = p.cnt["s"]
        t_vt = p.cnt["t"]
        for r in range(4):
            kvb = c * 4 + r
            st(_dap(SC["VTd"], kvb * 16 * 128 * 66,
                    [[66, 128], [128 * 66, 16], [1, 66]]),
               bass.AP(VTS.tensor, VTS.offset + r * 1056,
                       [[AB_N, 128], [66, 16], [1, 66]]),
               waits=[("s", scv)])
        MQROW = ROWS[0:1, 3584:3600]
        MXROW = ROWS[0:1, 3568:3584]
        mq_marks = {}
        for h in range(NH + 2):
            if h < NH:
                par = h % 2
                b0 = 64 * par
                qsl = QS[b0:b0 + 64, (h // 2) * 512:(h // 2) * 512 + 512]
                ksl = KA[b0:b0 + 64, (h // 2) * 4096 + c * 512:
                         (h // 2) * 4096 + c * 512 + 512]
                sqsl = SQB[b0:b0 + 64, 0:512]
                onesl = ONESCB[b0:b0 + 64, :]
                p.emit("scalar", lambda e, k=ksl, s=sqsl: e.activation(
                    s, k, AF.Square), waits=mq_marks.get(h - 2, []),
                    inc="s")
                sc = p.cnt["s"]
                p.emit("tensor", lambda e, s=sqsl, o=onesl: e.matmul(
                    PSB[0:1, 1536:2048], o, s, start=True, stop=True),
                    waits=[("s", sc)], inc="t")
                tc_ = p.cnt["t"]
                p.emit("scalar", lambda e: e.copy(K2R,
                                                  PSB[0:1, 1536:2048]),
                       waits=[("t", tc_)], inc="s")
                sc = p.cnt["s"]
                p.emit("vector", lambda e, h=h: e.tensor_reduce(
                    MXROW[0:1, h:h + 1], K2R, AX.X, ALU.max),
                    waits=[("s", sc)], inc="v")
                p.emit("scalar", lambda e, q=qsl, s=sqsl: e.activation(
                    s, q, AF.Square), waits=p.after("v"), inc="s")
                sc = p.cnt["s"]
                p.emit("tensor", lambda e, s=sqsl, o=onesl: e.matmul(
                    PSB[0:1, 1024:1536], o, s, start=True, stop=True),
                    waits=[("s", sc)], inc="t")
                tc_ = p.cnt["t"]
                mq_marks[h] = [("t", tc_)]
                p.emit("vector", lambda e, h=h: e.reduce_max(
                    MQROW[0:1, h:h + 1], PSB[0:1, 1024:1536], axis=AX.X),
                    waits=[("t", tc_)], inc="v")
            if NH > h >= 1:
                p.emit("vector", lambda e, h=h - 1: e.tensor_max(
                    MK2[0:1, h:h + 1], MK2[0:1, h:h + 1],
                    MXROW[0:1, h:h + 1]), inc="v")
            if NH + 1 > h >= 2:
                p.emit("vector", lambda e, h=h - 2: e.tensor_add(
                    NMR[0:1, h:h + 1], MQROW[0:1, h:h + 1],
                    MK2[0:1, h:h + 1]), inc="v")
            if h >= 3:
                p.emit("vector", lambda e, h=h - 3: e.tensor_single_scalar(
                    NMR[0:1, h:h + 1], NMR[0:1, h:h + 1], -1.0 / 16.0,
                    ALU.mult), inc="v")
        for h in (NH - 1,):
            p.emit("vector", lambda e, h=h: e.tensor_max(
                MK2[0:1, h:h + 1], MK2[0:1, h:h + 1], MXROW[0:1, h:h + 1]),
                inc="v")
        p.emit("vector", lambda e: e.tensor_copy(
            ROWS[0:1, 3560:3561], ROWS[0:1, 3561:3562]), inc="v")
        for h in (NH - 1,):
            p.emit("vector", lambda e, h=h: e.tensor_add(
                NMR[0:1, h:h + 1], MQROW[0:1, h:h + 1], MK2[0:1, h:h + 1]),
                inc="v")
        p.emit("vector", lambda e: e.tensor_copy(
            ROWS[0:1, 3560:3561], ROWS[0:1, 3561:3562]), inc="v")
        for h in (NH - 1,):
            p.emit("vector", lambda e, h=h: e.tensor_single_scalar(
                NMR[0:1, h:h + 1], NMR[0:1, h:h + 1], -0.5,
                ALU.mult), inc="v")
        vnm = p.cnt["v"]
        p.emit("tensor", lambda e: e.matmul(
            PSB[:, 1024:1040], ONESR, NMR[0:1, 0:16], start=True,
            stop=True), waits=[("v", vnm)], inc="t")
        tc_ = p.cnt["t"]
        p.emit("vector", lambda e: e.tensor_copy(NMC, PSB[:, 1024:1040]),
               waits=[("t", tc_)], inc="v")
        vnmc = p.cnt["v"]
        nkvb = 4 * (c + 1)
        oc0 = p.cnt["o"]
        last_e = {}
        mask_mark = {}
        pend = [None]
        for h in range(NH):
            par = h % 2
            b0 = 64 * par
            vtb = VTB2[par]
            ldw = [("o", oc0)] + av_t.get(c * NH + h - 2, [])
            if h == 1:
                ldw = [("o", oc0), ("t", t_vt)]
            ld(vtb[:, 0:nkvb * 66],
               _dap(SC["VTd"], h * 128 * 66,
                    [[66, 128], [16 * 128 * 66, nkvb], [1, 66]]),
               waits=ldw)
            dch = p.cnt["d"]
            if h == NH - 1:
                d_vtb_last[0] = p.cnt["d"]
            avp = (PSB[0:65, 1536:2048] if par == 0
                   else PSB[0:65, 1024:1536])
            t_sc = {}
            t_av = {}

            def emit_sc(kvb, h=h, b0=b0):
                bi = (0, 1, 3)[kvb % 3]
                bank = PSA[:, bi * 512: bi * 512 + 512]
                p.emit("tensor", lambda e, h=h, kvb=kvb, bk=bank,
                       b0=b0: e.matmul(
                           bk, KA[b0:b0 + 64, (h // 2) * 4096 + kvb
                                  * 128: (h // 2) * 4096 + kvb * 128
                                  + 128],
                           QS[b0:b0 + 64, (h // 2) * 512:
                              (h // 2) * 512 + 512],
                           start=True, stop=True),
                       waits=last_e.get(bi, [("s", scv)]), inc="t")
                t_sc[kvb] = [("t", p.cnt["t"])]
                return bank

            banks = {}
            for kvb in range(min(2, nkvb)):
                banks[kvb] = emit_sc(kvb)
            if c == 0 and pend[0] is not None:
                pend[0]()
                pend[0] = None
            for kvb in range(nkvb):
                bi = (0, 1, 3)[kvb % 3]
                bank = banks.pop(kvb)
                esl = EBUF[kvb % 2]
                ew = t_sc[kvb] + mask_mark.get(kvb % 2, [])
                if h == 0 and kvb == 0:
                    ew = ew + [("v", vnmc)]
                p.emit("scalar", lambda e, bk=bank, o=esl, h=h:
                       e.activation(o, bk, AF.Exp, bias=NMC[:, h:h + 1]),
                       waits=ew, inc="s")
                se = p.cnt["s"]
                last_e[bi] = [("s", se)]
                rhs = esl
                if kvb >= c * 4:
                    r = kvb - c * 4
                    p.emit("vector", lambda e, i=esl, r=r:
                           e.tensor_mul(EM, i,
                                        CMASK[:, (3 - r) * 128:(3 - r) * 128 + 512]),
                           waits=[("s", se)] + t_av.get(kvb - 1, [])
                           + p.after("o"), inc="v")
                    rhs = EM
                    mask_mark[kvb % 2] = [("v", p.cnt["v"])]
                p.emit("tensor", lambda e, kvb=kvb, rhs=rhs, vtb=vtb,
                       av=avp, nk=nkvb: e.matmul(
                           av, vtb[:, kvb * 66: kvb * 66 + 65], rhs,
                           start=(kvb == 0), stop=(kvb == nk - 1)),
                       waits=[("s", p.cnt["s"]), ("v", p.cnt["v"]),
                              ("d", dch)], inc="t")
                t_av[kvb] = [("t", p.cnt["t"])]
                if kvb == 0 and pend[0] is not None:
                    pend[0]()
                    pend[0] = None
                if kvb + 2 < nkvb:
                    banks[kvb + 2] = emit_sc(kvb + 2)
            tav = p.cnt["t"]
            av_t[c * NH + h] = [("t", tav)]
            p.emit("vector", lambda e, av=avp: e.reciprocal(
                RL, av[64:65, 0:512]), waits=[("t", tav)], inc="v")
            vrc = p.cnt["v"]

            def tail(h=h, par=par, avp=avp, vrc=vrc):
                p.emit("tensor", lambda e: e.matmul(
                    PSA[0:64, 1024:1536], ONESR[0:1, 0:64], RL,
                    start=True, stop=True), waits=[("v", vrc)], inc="t")
                tc_ = p.cnt["t"]
                p.emit("scalar", lambda e, av=avp: e.copy(
                    AVS, av[0:64, 0:512]), waits=[("t", tc_)], inc="s")
                sc = p.cnt["s"]
                if par == 0:
                    dst = ATO[0:64, (h // 2) * 512:(h // 2) * 512 + 512]
                    p.emit("vector", lambda e, d=dst: e.tensor_mul(
                        d, AVS, PSA[0:64, 1024:1536]),
                        waits=[("s", sc)], inc="v")
                else:
                    stg = EM[0:64, 0:512]
                    p.emit("vector", lambda e, d=stg: e.tensor_mul(
                        d, AVS, PSA[0:64, 1024:1536]),
                        waits=[("s", sc)] + p.after("o"), inc="v")
                    vc = p.cnt["v"]
                    st(bass.AP(ATO.tensor, ATO.offset + 64 * AB_N
                               + (h // 2) * 512,
                               [[AB_N, 64], [1, 512]]),
                       bass.AP(EM.tensor, EM.offset,
                               [[AB_N, 64], [1, 512]]),
                       waits=[("v", vc)])
            pend[0] = tail
        pend[0]()
        pend[0] = None
        oc = p.cnt["o"]
        vato = p.cnt["v"]
        for db in range(8):
            bank = PSA[:, (db % 4) * 512:((db % 4) + 1) * 512]
            for kb in range(8):
                p.emit("tensor", lambda e, db=db, kb=kb, bk=bank: e.matmul(
                    bk, OWT[:, kb * 1024 + db * 128: kb * 1024 + db * 128
                            + 128], ATO[:, kb * 512:(kb + 1) * 512],
                    start=(kb == 0), stop=(kb == 7)),
                    waits=(([("o", oc), ("v", vato)] + p.after("s"))
                           if kb == 0 else ()),
                    inc="t" if kb == 7 else None)
            tc_ = p.cnt["t"]
            p.emit("vector", lambda e, db=db, bk=bank: e.tensor_add(
                X2[:, db * 512:(db + 1) * 512],
                XI[:, db * 512:(db + 1) * 512], bk),
                waits=[("t", tc_)], inc="v")
        vc = p.cnt["v"]
        st(_dap(SC["x2F"], c * TC, [[T, 128], [128 * T, 8], [1, TC]]),
           X2, waits=[("v", vc)])
        p.phase()


def _moe(H):
    p, AB, AFa = H["p"], H["AB"], H["AFa"]
    ROWS, PSA, PSB = H["ROWS"], H["PSA"], H["PSB"]
    IN, SC, WBF = H["IN"], H["SC"], H["WBF"]
    ld, st, prm = H["ld"], H["st"], H["prm"]
    IDF, IDB, ONESR, ONESCB = H["IDF"], H["IDB"], H["ONESR"], H["ONESCB"]

    DWN = AB[:, 0:32768]
    UPW = AB[:, 32768:49152]
    UPO = AB[:, 49152:65536]
    LN2 = AB[:, 65536:69632]
    GS = AB[0:8, 69632:70144]
    FRSC = AB[:, 70144:70656]
    GATET = AB[:, 70656:70720]
    SQM = LN2
    XC2 = AFa[:, 0:4096]
    TMQ = AFa[:, 4096:4608]
    GPT = AFa[:, 4608:4640]
    EG = AFa[:, 4640:4672]
    EMK = AFa[:, 4672:4704]
    COLS = AFa[:, 4704:4770]
    X3 = AFa[:, 4772:8868]
    GLD = AFa[:, 0:64]

    ld(DWN, _dap(WBF["downT"], 0, [[D, 128], [128 * D, 32], [1, 1024]]),
       waits=p.after())
    ld(GLD, _dap(IN["gateT"], 0, [[NEX, 128], [128 * NEX, 8], [1, NEX]]))
    dc = p.cnt["d"]
    p.emit("vector", lambda e: e.tensor_copy(GATET, GLD),
           waits=[("d", dc)], inc="v")
    n2 = prm("n2")
    p.phase()

    xc2_free = p.after("v", "s", "t")
    upw_free = p.after("t")
    x3st = []
    upsil = {}
    for c in range(NCH):
        ld(XC2, _dap(SC["x2F"], c * TC, [[T, 128], [128 * T, 8],
                                         [1, TC]]),
           waits=xc2_free)
        dc = p.cnt["d"]
        ld(UPW, _dap(WBF["upT"], 0,
                     [[ED, 128], [128 * ED, 8], [1, 2048]]),
           waits=upw_free)
        dup = p.cnt["d"]
        p.emit("scalar", lambda e: e.activation(SQM, XC2, AF.Square),
               waits=[("d", dc), ("o", p.cnt["o"])] + upw_free, inc="s")
        rstd = ROWS[0:1, 2048:2560]
        _ln_stats(H, None, SQM, rstd, rms=True, eps=EPS_RMS)
        sc = p.cnt["s"]
        p.emit("tensor", lambda e: e.matmul(
            PSA[:, 0:512], ONESR, rstd, start=True, stop=True),
            waits=[("s", sc)], inc="t")
        tc_ = p.cnt["t"]
        for i in range(8):
            p.emit("vector", lambda e, i=i: e.tensor_mul(
                TMQ, XC2[:, i * 512:(i + 1) * 512], PSA[:, 0:512]),
                waits=[("t", tc_)], inc="v")
            p.emit("vector", lambda e, i=i: e.tensor_scalar_mul(
                LN2[:, i * 512:(i + 1) * 512], TMQ, n2[:, i: i + 1]),
                inc="v")
        vln = p.cnt["v"]
        for kb in range(8):
            p.emit("tensor", lambda e, kb=kb: e.matmul(
                PSB[0:8, 1024:1536], GATET[:, kb * 8:(kb + 1) * 8],
                LN2[:, kb * 512:(kb + 1) * 512],
                start=(kb == 0), stop=(kb == 7)),
                waits=[("v", vln)] if kb == 0 else (),
                inc="t" if kb == 7 else None)
        tc_ = p.cnt["t"]
        p.emit("scalar", lambda e: e.copy(GS, PSB[0:8, 1024:1536]),
               waits=[("t", tc_)], inc="s")
        sc = p.cnt["s"]
        gptm = p.after("v")
        for r in range(4):
            ptb = PSA[:, 1024:1536].bitcast(BF16)[0:128, 0:8]
            p.emit("tensor", lambda e, r=r, o=ptb: e.transpose(
                o, GS[:, r * 128:(r + 1) * 128], IDB[0:8, 0:8]),
                waits=([("s", sc)] if r == 0 else []) + gptm, inc="t")
            tc_ = p.cnt["t"]
            p.emit("vector", lambda e, r=r, i=ptb: e.tensor_copy(
                GPT[:, r * 8:(r + 1) * 8], i), waits=[("t", tc_)],
                inc="v")
            gptm = [("v", p.cnt["v"])]
            vc = p.cnt["v"]
            mx = COLS[:, r: r + 1]
            p.emit("vector", lambda e, r=r, mx=mx: e.reduce_max(
                mx, GPT[:, r * 8:(r + 1) * 8], axis=AX.X),
                waits=[("t", p.cnt["t"])], inc="v")
            vc = p.cnt["v"]
            p.emit("gpsimd", lambda e, r=r, mx=mx: e.tensor_single_scalar(
                EG[:, r * 8:(r + 1) * 8], GPT[:, r * 8:(r + 1) * 8],
                mx, ALU.subtract), waits=[("v", vc)], inc="g")
            gc = p.cnt["g"]
            p.emit("scalar", lambda e, r=r: e.activation(
                EG[:, r * 8:(r + 1) * 8], EG[:, r * 8:(r + 1) * 8],
                AF.Exp), waits=[("g", gc)], inc="s")
            sc2 = p.cnt["s"]
            tsum = COLS[:, 4 + r: 5 + r]
            m1 = COLS[:, 8 + r: 9 + r]
            m2c = COLS[:, 12 + r: 13 + r]
            p.emit("vector", lambda e, t=tsum, r=r: e.reduce_sum(
                t, EG[:, r * 8:(r + 1) * 8], axis=AX.X),
                waits=[("s", sc2)], inc="v")
            p.emit("vector", lambda e, m=m1, r=r: e.reduce_max(
                m, EG[:, r * 8:(r + 1) * 8], axis=AX.X), inc="v")
            vc = p.cnt["v"]
            p.emit("gpsimd", lambda e, r=r, m=m1: e.tensor_scalar(
                EMK[:, r * 8:(r + 1) * 8], EG[:, r * 8:(r + 1) * 8],
                m, None, ALU.is_ge), waits=[("v", vc)], inc="g")
            p.emit("gpsimd", lambda e, r=r: e.tensor_scalar(
                EMK[:, r * 8:(r + 1) * 8], EMK[:, r * 8:(r + 1) * 8],
                -1.0, 1.0, ALU.mult, ALU.add), inc="g")
            p.emit("gpsimd", lambda e, r=r: e.tensor_mul(
                EMK[:, r * 8:(r + 1) * 8], EMK[:, r * 8:(r + 1) * 8],
                EG[:, r * 8:(r + 1) * 8]), inc="g")
            gc = p.cnt["g"]
            p.emit("vector", lambda e, m=m2c, r=r: e.reduce_max(
                m, EMK[:, r * 8:(r + 1) * 8], axis=AX.X),
                waits=[("g", gc)], inc="v")
            s12 = COLS[:, 16 + r: 17 + r]
            den = COLS[:, 20 + r: 21 + r]
            rden = COLS[:, 24 + r: 25 + r]
            fac = COLS[:, 28 + r: 29 + r]
            spc = COLS[:, 32:33]
            p.emit("vector", lambda e, s=spc: e.tensor_copy(s, s), inc="v")
            p.emit("vector", lambda e, m=m1, m2=m2c, s=s12:
                   e.tensor_add(s, m, m2), inc="v")
            p.emit("vector", lambda e, s=spc: e.tensor_copy(s, s), inc="v")
            p.emit("vector", lambda e, t=tsum, s=s12, d=den:
                   e.scalar_tensor_tensor(d, t, 1e-9, s, ALU.mult,
                                          ALU.add), inc="v")
            p.emit("vector", lambda e, s=spc: e.tensor_copy(s, s), inc="v")
            p.emit("vector", lambda e, d=den, rd=rden: e.reciprocal(
                rd, d), inc="v")
            p.emit("vector", lambda e, s=spc: e.tensor_copy(s, s), inc="v")
            p.emit("vector", lambda e, s=s12, rd=rden, f=fac:
                   e.tensor_mul(f, s, rd), inc="v")
            vc = p.cnt["v"]
            ptf = PSA[:, 1536:1664]
            p.emit("tensor", lambda e, f=fac, o=ptf: e.transpose(
                o[0:1, 0:128], f, IDF), waits=[("v", vc)], inc="t")
            tc_ = p.cnt["t"]
            p.emit("scalar", lambda e, r=r, i=ptf: e.copy(
                ROWS[0:1, 3072 + r * 128:3072 + (r + 1) * 128], i[0:1, 0:128]),
                waits=[("t", tc_)], inc="s")
        sgate = p.cnt["s"]
        for ebh in range(2):
            if ebh == 1:
                ld(UPW, _dap(WBF["upT"], 2048,
                             [[ED, 128], [128 * ED, 8], [1, 2048]]),
                   waits=[("t", p.cnt["t"])])
                dup = p.cnt["d"]
            for eb in range(16):
                bank = PSB[:, (eb % 2) * 512:((eb % 2) + 1) * 512]
                for kb in range(8):
                    p.emit("tensor", lambda e, eb=eb, kb=kb, bk=bank:
                           e.matmul(bk, UPW[:, kb * 2048 + eb * 128:
                                            kb * 2048 + eb * 128 + 128],
                                    LN2[:, kb * 512:(kb + 1) * 512],
                                    start=(kb == 0), stop=(kb == 7)),
                           waits=(([("d", dup)]
                                   + upsil.get(eb % 2, p.after("s")))
                                  if kb == 0 else ()),
                           inc="t" if kb == 7 else None)
                tc_ = p.cnt["t"]
                p.emit("scalar", lambda e, ebh=ebh, eb=eb, bk=bank:
                       e.activation(UPO[:, (ebh * 16 + eb) * 512:
                                        (ebh * 16 + eb) * 512 + 512],
                                    bk, AF.Silu),
                       waits=[("t", tc_)], inc="s")
                upsil[eb % 2] = [("s", p.cnt["s"])]
        upw_free = [("t", p.cnt["t"])]
        p.emit("tensor", lambda e: e.matmul(
            PSA[:, 1024:1536], ONESR, ROWS[0:1, 3072:3584], start=True,
            stop=True), waits=[("s", sgate)] + gptm, inc="t")
        tc_ = p.cnt["t"]
        p.emit("scalar", lambda e: e.copy(FRSC, PSA[:, 1024:1536]),
               waits=[("t", tc_)] + (xc2_free if c > 0 else []),
               inc="s")
        sfr = p.cnt["s"]
        sup = p.cnt["s"]
        for db in range(8):
            bank = PSA[:, (db % 4) * 512:((db % 4) + 1) * 512]
            for kb in range(32):
                p.emit("tensor", lambda e, db=db, kb=kb, bk=bank: e.matmul(
                    bk, DWN[:, kb * 1024 + db * 128: kb * 1024 + db * 128
                            + 128], UPO[:, kb * 512:(kb + 1) * 512],
                    start=(kb == 0), stop=(kb == 31)),
                    waits=(([("s", sup)] + p.after("v")) if kb == 0
                           else ()),
                    inc="t" if kb == 31 else None)
            tc_ = p.cnt["t"]
            p.emit("vector", lambda e, db=db, bk=bank: e.tensor_mul(
                TMQ, bk, FRSC), waits=[("t", tc_), ("s", sfr)], inc="v")
            p.emit("vector", lambda e, db=db: e.tensor_add(
                X3[:, db * 512:(db + 1) * 512],
                XC2[:, db * 512:(db + 1) * 512], TMQ),
                waits=x3st if db == 0 else (), inc="v")
        vc = p.cnt["v"]
        xc2_free = [("v", vc)]
        st(_dap(SC["x2F"], c * TC, [[T, 128], [128 * T, 8], [1, TC]]),
           X3, waits=[("v", vc)])
        x3st = [("o", p.cnt["o"])]
    p.phase()


def _exit(H, OUT):
    p, AFa, PSA = H["p"], H["AFa"], H["PSA"]
    SC, ld, st, IDF = H["SC"], H["ld"], H["st"], H["IDF"]
    A1 = AFa[:, 0:4096]
    A3 = AFa[:, 4096:8192]
    OST = AFa[:, 4096:5120]
    OST2 = AFa[:, 5120:6144]

    ost_mark = {}
    st_mark = {}
    for c in range(NCH):
        ld(A1, _dap(SC["xF"], c * TC, [[T, 128], [128 * T, 8], [1, TC]]),
           waits=p.after("v", "s", "t", "o"))
        ld(A3, _dap(SC["x2F"], c * TC, [[T, 128], [128 * T, 8],
                                        [1, TC]]))
        dc = p.cnt["d"]
        for i in range(8):
            p.emit("vector", lambda e, i=i: e.tensor_add(
                A1[:, i * 512:(i + 1) * 512],
                A1[:, i * 512:(i + 1) * 512],
                A3[:, i * 512:(i + 1) * 512]),
                waits=[("d", dc), ("o", p.cnt["o"])], inc="v")
        vc = p.cnt["v"]
        for r in range(4):
            ob = (OST, OST2)[r % 2]
            for i in range(8):
                k = (r * 8 + i) % 4
                bank = PSA[:, k * 512: k * 512 + 128]
                p.emit("tensor", lambda e, r=r, i=i, o=bank: e.transpose(
                    o, A1[:, i * 512 + r * 128: i * 512 + r * 128 + 128],
                    IDF),
                    waits=[("v", vc)] + ost_mark.get(k, p.after("s")),
                    inc="t")
                tc_ = p.cnt["t"]
                p.emit("scalar", lambda e, i=i, o=bank, ob=ob: e.copy(
                    ob[:, i * 128:(i + 1) * 128], o),
                    waits=[("t", tc_)]
                    + (st_mark.get(r % 2, p.after("o")) if i == 0
                       else []), inc="s")
                ost_mark[k] = [("s", p.cnt["s"])]
            sc = p.cnt["s"]
            st(_dap(OUT, (c * TC + r * 128) * D, [[D, 128], [1, D]]),
               ob[:, 0:1024], waits=[("s", sc)])
            st_mark[r % 2] = [("o", p.cnt["o"])]


def kernel(**inputs):
    consts = _prep_consts()
    nc = build()
    in_maps = []
    for b in range(8):
        m = _prep_core(inputs, b)
        m.update(consts)
        in_maps.append(m)
    res = run_bass_kernel_spmd(nc, in_maps, core_ids=list(range(8)))
    out = np.stack([np.asarray(r["out"]) for r in res.results], axis=0)
    return out.astype(np.float32)
